# version 09
# baseline (speedup 1.0000x reference)
import numpy as np
import ml_dtypes
import contextlib
import concourse.bass as bass
import concourse.mybir as mybir
from concourse.bass_utils import run_bass_kernel_spmd

F32 = mybir.dt.float32
BF16 = mybir.dt.bfloat16
I16 = mybir.dt.int16
AF = mybir.ActivationFunctionType
ALU = mybir.AluOpType
AX = mybir.AxisListType
BFNP = ml_dtypes.bfloat16

D = 4096
DC = 32
WD = 1024
H = 16
BL = 128
T = 1024
TH = 1026
NB = 8
NCORES = 8
EPS = 1e-5
INV_D = 1.0 / D
QSCALE = 0.125
PIPE_LN = True
PREFETCH_W = True
PF_EARLY = False
PF_DIST = 2

W3G = [(0, 512), (512, 512), (1024, 2)]
W2G = [(1, 512), (513, 512)]
CG = [(0, 512), (512, 512)]

OFF_WBTB = [0, 4096, 8192, 12288, 16384, 20480]
OFF_ICS = 24576
OFF_MROW = 32768
OFF_VROW = 34816
OFF_RROW = 36864
OFF_MRROW = 38912
OFF_RBFROW = 40960
OFF_MRBFROW = 41984
OFF_RBT = [43008, 44032, 45056]
OFF_MRBT = [46080, 47104, 48128]
OFF_SQ = [57344, 58368, 59392, 60416]
OFF_TMPA = [61440, 62464, 63488, 64512]
OFF_OSH = [32768, 34816]
OFF_TSB = [36864, 38912]
OFF_T2 = [45056, 47104]
OFF_LG = 49152
OFF_PBUF = [51200, 51456]
OFF_RCOLS = 51712
OFF_QKV = 0
OFF_CWR = [49152 + i * 2048 for i in range(4)]
OFF_VHT = 57344
OFF_PTS = 59392
OFF_PN = [63488 + i * 256 for i in range(4)]
OFF_ACOL = 64512
OFF_OSTAGE = [49152, 57344]


class Sched:
    def __init__(self):
        self.ev = {k: [] for k in ("sync", "tensor", "vector", "scalar", "gpsimd")}
        self.cnt = {}
        self.audit = []
        self.producer = {}

    def val(self, sem):
        return self.cnt.get(id(sem), 0)

    def bump(self, sem, by):
        self.cnt[id(sem)] = self.cnt.get(id(sem), 0) + by
        return self.cnt[id(sem)]

    def op(self, eng, fn, waits=(), incs=()):
        wl = tuple((s, v) for (s, v) in waits if v > 0)
        for (s, v) in wl:
            self.audit.append((eng, id(s), v, len(self.ev[eng])))
        after = [self.bump(s, by) for (s, by) in incs]
        il = tuple(incs)
        for (s, by) in il:
            self.producer.setdefault(id(s), eng)

        def run(e, fn=fn, wl=wl, il=il):
            for (s, v) in wl:
                e.wait_ge(s, v)
            inst = fn(e)
            for (s, by) in il:
                inst = inst.then_inc(s, by)

        self.ev[eng].append(run)
        return after

    def check(self, names):
        bad = 0
        for (eng, sid, v, pos) in self.audit:
            if v > self.cnt.get(sid, 0):
                print(f"AUDIT: {eng} event#{pos} waits {names.get(sid,'?')}>={v} "
                      f"but final={self.cnt.get(sid,0)}")
                bad += 1
        if bad:
            raise AssertionError(f"{bad} unreachable waits")


def _build_program(upto=99, tap=None):
    nc = bass.Bass("TRN2", target_bir_lowering=False)

    x_in = nc.dram_tensor("x_in", [TH, D], F32, kind="ExternalInput").ap()
    hmrow = nc.dram_tensor("hmrow", [1, TH], F32, kind="ExternalInput").ap()
    conv_w = nc.dram_tensor("conv_w", [3, D, 3072], F32, kind="ExternalInput").ap()
    convb_c = nc.dram_tensor("convb_c", [128, 24], F32, kind="ExternalInput").ap()
    wq = nc.dram_tensor("wq", [WD, WD], F32, kind="ExternalInput").ap()
    wk = nc.dram_tensor("wk", [WD, WD], F32, kind="ExternalInput").ap()
    wv = nc.dram_tensor("wv", [WD, WD], F32, kind="ExternalInput").ap()
    wo = nc.dram_tensor("wo", [WD, WD], F32, kind="ExternalInput").ap()
    bq_c = nc.dram_tensor("bq_c", [128, 8], F32, kind="ExternalInput").ap()
    bk_c = nc.dram_tensor("bk_c", [128, 8], F32, kind="ExternalInput").ap()
    bv_c = nc.dram_tensor("bv_c", [128, 8], F32, kind="ExternalInput").ap()
    bo_c = nc.dram_tensor("bo_c", [128, 8], F32, kind="ExternalInput").ap()
    projw = nc.dram_tensor("projw", [WD, D], F32, kind="ExternalInput").ap()
    projb_c = nc.dram_tensor("projb_c", [128, 32], F32, kind="ExternalInput").ap()
    fwT = nc.dram_tensor("fwT", [D, D], F32, kind="ExternalInput").ap()
    fb_c = nc.dram_tensor("fb_c", [128, 32], F32, kind="ExternalInput").ap()
    gwT3 = nc.dram_tensor("gwT3", [128, 256], F32, kind="ExternalInput").ap()
    ics = nc.dram_tensor("ics", [8, D], F32, kind="ExternalInput").ap()
    ocs = nc.dram_tensor("ocs", [8, D], F32, kind="ExternalInput").ap()
    x_out = nc.dram_tensor("x_out", [T, D], F32, kind="ExternalOutput").ap()

    scr_fw = nc.dram_tensor("scr_fw", [64 * 128, 2048], BF16, kind="Internal").ap()

    identB = nc.inline_tensor(np.eye(128, dtype=BFNP), "identB").ap()
    identF = nc.inline_tensor(np.eye(128, dtype=np.float32), "identF").ap()
    ones_col = nc.inline_tensor(np.ones((128, 1), dtype=BFNP), "ones_col").ap()
    ones_row = nc.inline_tensor(np.ones((1, 128), dtype=BFNP), "ones_row").ap()
    mask_np = np.tril(np.ones((128, 128))).astype(BFNP)
    binmask = nc.inline_tensor(mask_np, "binmask").ap()
    epsc = nc.inline_tensor(np.full((1, 1), EPS, dtype=np.float32), "epsc").ap()

    with contextlib.ExitStack() as st:
        XR = st.enter_context(nc.sbuf_tensor("XR", [128, 32 * 1040], BF16))
        XN = st.enter_context(nc.sbuf_tensor("XN", [128, 32 * TH], BF16))
        ARENA = st.enter_context(nc.sbuf_tensor("ARENA", [128, 32768], BF16))
        ROUT = st.enter_context(nc.sbuf_tensor("ROUT", [8, TH], BF16))
        sb_convb = st.enter_context(nc.sbuf_tensor("sb_convb", [128, 24], F32))
        sb_bq = st.enter_context(nc.sbuf_tensor("sb_bq", [128, 8], F32))
        sb_bk = st.enter_context(nc.sbuf_tensor("sb_bk", [128, 8], F32))
        sb_bv = st.enter_context(nc.sbuf_tensor("sb_bv", [128, 8], F32))
        sb_bo = st.enter_context(nc.sbuf_tensor("sb_bo", [128, 8], F32))
        sb_pb = st.enter_context(nc.sbuf_tensor("sb_pb", [128, 32], F32))
        sb_fbh = st.enter_context(nc.sbuf_tensor("sb_fbh", [128, 32], F32))
        sb_gwT = st.enter_context(nc.sbuf_tensor("sb_gwT", [128, 256], BF16))
        sb_id = st.enter_context(nc.sbuf_tensor("sb_id", [128, 128], BF16))
        sb_idf = st.enter_context(nc.sbuf_tensor("sb_idf", [128, 128], F32))
        sb_ones = st.enter_context(nc.sbuf_tensor("sb_ones", [128, 1], BF16))
        sb_onesr = st.enter_context(nc.sbuf_tensor("sb_onesr", [1, 128], BF16))
        sb_mask = st.enter_context(nc.sbuf_tensor("sb_mask", [128, 128], BF16))
        sb_hm = st.enter_context(nc.sbuf_tensor("sb_hm", [1, TH], F32))
        sb_eps = st.enter_context(nc.sbuf_tensor("sb_eps", [1, 1], F32))

        PS = [st.enter_context(nc.psum_tensor(f"ps{i}", [128, 512], F32)) for i in range(8)]

        sW = st.enter_context(nc.semaphore("sW"))
        sHW = st.enter_context(nc.semaphore("sHW"))
        sPE = st.enter_context(nc.semaphore("sPE"))
        sDV = st.enter_context(nc.semaphore("sDV"))
        sAC = st.enter_context(nc.semaphore("sAC"))
        sPL = st.enter_context(nc.semaphore("sPL"))
        sOUT = st.enter_context(nc.semaphore("sOUT"))

        S = Sched()

        def snap():
            return [(sPE, S.val(sPE)), (sDV, S.val(sDV)), (sAC, S.val(sAC)),
                    (sPL, S.val(sPL))]

        ARB = ARENA.ap()
        ARF = ARB.bitcast(F32)

        def arb(off, n):
            return ARB[:, off // 2: off // 2 + n]

        def arf(off, n):
            return ARF[:, off // 4: off // 4 + n]

        def arbp(off, p, n):
            return ARB[0:p, off // 2: off // 2 + n]

        def arfp(off, p, n):
            return ARF[0:p, off // 4: off // 4 + n]

        XRa = XR.ap()
        XNa = XN.ap()
        ROUTa = ROUT.ap()

        def XRs(c, g0, n):
            return XRa[:, c * 1040 + g0: c * 1040 + g0 + n]

        def XNs(c, g0, n):
            return XNa[:, c * TH + g0: c * TH + g0 + n]

        def QKVs(c, g0, n):
            return ARB[:, c * 1024 + g0: c * 1024 + g0 + n]

        pb_bf = PS[7].ap().bitcast(BF16)
        pb_bf6 = PS[6].ap().bitcast(BF16)
        icsb = ARB[0:8, OFF_ICS // 2: OFF_ICS // 2 + D]

        ln_sq_free = [[], [], [], []]
        ln_tmp_free = [[], [], [], []]
        ln_bt_free = [[], [], []]
        ln_ps0_free = [[]]
        ln_ps1_free = [[]]

        for (dst, src) in [(sb_convb.ap(), convb_c), (sb_bq.ap(), bq_c),
                           (sb_bk.ap(), bk_c), (sb_bv.ap(), bv_c), (sb_bo.ap(), bo_c),
                           (sb_pb.ap(), projb_c), (sb_fbh.ap(), fb_c),
                           (sb_id.ap(), identB), (sb_idf.ap(), identF),
                           (sb_ones.ap(), ones_col), (sb_onesr.ap(), ones_row),
                           (sb_mask.ap(), binmask), (sb_hm.ap(), hmrow),
                           (sb_eps.ap(), epsc)]:
            S.op("sync", lambda e, d=dst, s=src: e.dma_start(out=d, in_=s),
                 incs=[(sHW, 16)])
        m_small = S.val(sHW)
        m_gw = S.op("gpsimd", lambda e: e.dma_start(out=sb_gwT.ap(), in_=gwT3),
                    incs=[(sW, 16)])[0]
        m_fbh = S.op("scalar", lambda e: e.mul(sb_fbh.ap(), sb_fbh.ap(), 0.5),
                     waits=[(sHW, m_small)], incs=[(sAC, 1)])[0]
        m_bqs = S.op("scalar", lambda e: e.mul(sb_bq.ap(), sb_bq.ap(), QSCALE),
                     incs=[(sAC, 1)])[0]
        m_const = [(sHW, m_small), (sAC, m_bqs), (sW, m_gw)]

        tm_free = [[], []]
        xr_ready = [None] * DC
        TTILES = [(t * 128, 128) for t in range(8)] + [(1024, 2)]
        pb_free_in = [[], []]
        for ti, (r0, nr) in enumerate(TTILES):
            tmv = ARB[0:128, (ti % 2) * 4096: (ti % 2) * 4096 + 4096]
            dv_ = S.op("gpsimd", lambda e, r0=r0, nr=nr, tmv=tmv:
                       e.dma_start(out=tmv[0:nr, :], in_=x_in[r0:r0 + nr, :]),
                       waits=list(tm_free[ti % 2]), incs=[(sW, 16)])[0]
            lastT = None
            for fcg in range(4):
                pbv = pb_bf if fcg % 2 == 0 else pb_bf6
                for j in range(8):
                    fc = fcg * 8 + j
                    w = []
                    if j == 0:
                        w += list(pb_free_in[fcg % 2])
                        if fcg == 0:
                            w.append((sW, dv_))
                        if ti == 0 and fcg == 0:
                            w.append((sHW, m_small))

                    def ft(e, tmv=tmv, nr=nr, fc=fc, j=j, pbv=pbv):
                        return e.transpose(pbv[:, j * 128: j * 128 + nr],
                                           tmv[0:nr, fc * 128:(fc + 1) * 128],
                                           sb_id.ap()[0:nr, 0:nr])
                    r = S.op("tensor", ft, waits=w,
                             incs=[(sPE, 1)] if j == 7 else [])
                    if j == 7:
                        lastT = r[0]
                for j in range(8):
                    fc = fcg * 8 + j

                    def fe(e, fc=fc, j=j, r0=r0, nr=nr, pbv=pbv):
                        return e.activation(
                            XRa[:, fc * 1040 + r0: fc * 1040 + r0 + nr],
                            pbv[:, j * 128: j * 128 + nr], AF.Copy)
                    ev_ = S.op("scalar", fe, waits=[(sPE, lastT)] if j == 0 else [],
                               incs=[(sAC, 1)])[0]
                    if ti == len(TTILES) - 1:
                        xr_ready[fc] = (sAC, ev_)
                pb_free_in[fcg % 2] = [(sAC, S.val(sAC))]
            tm_free[ti % 2] = [(sPE, lastT)]


        ln_group_done_tmp = []

        def _emit_applies(ev, gi):
            (g0, n, rbt, mrbt, e1, e2) = ev
            for c in range(DC):
                tA = arb(OFF_TMPA[c % 4], 512)[:, 0:n]
                w = [(sAC, e1)] + list(ln_tmp_free[c % 4])
                vv = S.op("vector", lambda e, tA=tA, c=c, g0=g0, n=n, rbt=rbt:
                          e.tensor_mul(tA, XRs(c, g0, n), rbt),
                          waits=w, incs=[(sDV, 1)])[0]
                pv = S.op("gpsimd", lambda e, tA=tA, c=c, g0=g0, n=n, mrbt=mrbt:
                          e.tensor_sub(XNs(c, g0, n), tA, mrbt),
                          waits=[(sDV, vv), (sAC, e2)], incs=[(sPL, 1)])[0]
                ln_tmp_free[c % 4] = [(sPL, pv)]
            ln_bt_free[gi] = [(sPL, S.val(sPL))]
            ln_group_done_tmp.append((sPL, S.val(sPL)))

        def emit_ln(hmask, entry_waits, xr_chunk_ready=None):
            ln_group_done_tmp.clear()
            evs = []
            for gi, (g0, n) in enumerate(W3G):
                for c in range(DC):
                    w = []
                    if gi == 0 and c == 0:
                        w += list(entry_waits)
                    if xr_chunk_ready is not None and gi == 0:
                        w.append(xr_chunk_ready[c])
                    if c == 0:
                        w += list(ln_ps0_free[0])

                    def f(e, c=c, g0=g0, n=n, first=(c == 0), last=(c == DC - 1)):
                        return e.matmul(PS[0].ap()[0:1, 0:n], sb_ones.ap(),
                                        XRs(c, g0, n), start=first, stop=last)
                    S.op("tensor", f, waits=w,
                         incs=[(sPE, 1)] if c == DC - 1 else [])
                m_sum = S.val(sPE)
                for c in range(DC):
                    sq = arb(OFF_SQ[c % 4], 512)[:, 0:n]
                    w = list(ln_sq_free[c % 4])
                    if gi == 0 and c <= 1:
                        w += list(entry_waits)
                    if xr_chunk_ready is not None and gi == 0:
                        w.append(xr_chunk_ready[c])
                    sq_eng = "vector" if c % 2 == 0 else "gpsimd"
                    sq_sem = sDV if c % 2 == 0 else sPL

                    def fs(e, c=c, g0=g0, n=n, sq=sq):
                        return e.tensor_mul(sq, XRs(c, g0, n), XRs(c, g0, n))
                    vs = S.op(sq_eng, fs, waits=w, incs=[(sq_sem, 1)])[0]

                    wm = [(sq_sem, vs)]
                    if c == 0:
                        wm += list(ln_ps1_free[0])

                    def fm(e, sq=sq, n=n, first=(c == 0), last=(c == DC - 1)):
                        return e.matmul(PS[1].ap()[0:1, 0:n], sb_ones.ap(), sq,
                                        start=first, stop=last)
                    vm = S.op("tensor", fm, waits=wm, incs=[(sPE, 1)])[0]
                    ln_sq_free[c % 4] = [(sPE, vm)]
                m_ssq = S.val(sPE)

                m_row = arfp(OFF_MROW, 1, n)
                v_row = arfp(OFF_VROW, 1, n)
                r_row = arfp(OFF_RROW, 1, n)
                mr_row = arfp(OFF_MRROW, 1, n)
                rbf = arbp(OFF_RBFROW, 1, n)
                mrbf = arbp(OFF_MRBFROW, 1, n)
                v1 = S.op("vector", lambda e, m_row=m_row, n=n:
                          e.tensor_scalar_mul(m_row, PS[0].ap()[0:1, 0:n], INV_D),
                          waits=[(sPE, m_sum)], incs=[(sDV, 1)])[0]
                ln_ps0_free[0] = [(sDV, v1)]
                v2 = S.op("vector", lambda e, mr_row=mr_row, m_row=m_row:
                          e.tensor_mul(mr_row, m_row, m_row),
                          waits=[(sDV, v1)], incs=[(sDV, 1)])[0]
                v3 = S.op("vector", lambda e, v_row=v_row, mr_row=mr_row, n=n:
                          e.scalar_tensor_tensor(v_row, PS[1].ap()[0:1, 0:n], INV_D,
                                                 mr_row, ALU.mult, ALU.subtract),
                          waits=[(sDV, v2), (sPE, m_ssq)], incs=[(sDV, 1)])[0]
                ln_ps1_free[0] = [(sDV, v3)]
                a1 = S.op("scalar", lambda e, mr_row=mr_row, v_row=v_row:
                          e.activation(mr_row, v_row, AF.Sqrt, bias=sb_eps.ap()),
                          waits=[(sDV, v3)], incs=[(sAC, 1)])[0]
                v4 = S.op("vector", lambda e, r_row=r_row, mr_row=mr_row:
                          e.reciprocal(r_row, mr_row),
                          waits=[(sAC, a1)], incs=[(sDV, 1)])[0]
                if hmask:
                    v4 = S.op("vector", lambda e, r_row=r_row, g0=g0, n=n:
                              e.tensor_mul(r_row, r_row, sb_hm.ap()[0:1, g0:g0 + n]),
                              waits=[(sDV, v4)], incs=[(sDV, 1)])[0]
                v5 = S.op("vector", lambda e, mr_row=mr_row, m_row=m_row, r_row=r_row:
                          e.tensor_mul(mr_row, m_row, r_row),
                          waits=[(sDV, v4)], incs=[(sDV, 1)])[0]
                v6 = S.op("vector", lambda e, rbf=rbf, r_row=r_row:
                          e.tensor_copy(rbf, r_row),
                          waits=[(sDV, v5)], incs=[(sDV, 1)])[0]
                v7 = S.op("vector", lambda e, mrbf=mrbf, mr_row=mr_row:
                          e.tensor_copy(mrbf, mr_row),
                          waits=[(sDV, v6)], incs=[(sDV, 1)])[0]
                p1 = S.op("tensor", lambda e, n=n, rbf=rbf:
                          e.matmul(PS[2].ap()[:, 0:n], sb_onesr.ap(), rbf,
                                   start=True, stop=True),
                          waits=[(sDV, v7), (sAC, S.val(sAC))], incs=[(sPE, 1)])[0]
                p2 = S.op("tensor", lambda e, n=n, mrbf=mrbf:
                          e.matmul(PS[3].ap()[:, 0:n], sb_onesr.ap(), mrbf,
                                   start=True, stop=True),
                          incs=[(sPE, 1)])[0]
                rbt = arb(OFF_RBT[gi], 512)[:, 0:n]
                mrbt = arb(OFF_MRBT[gi], 512)[:, 0:n]
                e1 = S.op("scalar", lambda e, rbt=rbt, n=n:
                          e.activation(rbt, PS[2].ap()[:, 0:n], AF.Copy),
                          waits=[(sPE, p1)] + list(ln_bt_free[gi]), incs=[(sAC, 1)])[0]
                e2 = S.op("scalar", lambda e, mrbt=mrbt, n=n:
                          e.activation(mrbt, PS[3].ap()[:, 0:n], AF.Copy),
                          waits=[(sPE, p2)], incs=[(sAC, 1)])[0]
                evs.append((g0, n, rbt, mrbt, e1, e2))
                if gi >= 1:
                    _emit_applies(evs[gi - 1], gi - 1)
            _emit_applies(evs[len(W3G) - 1], len(W3G) - 1)
            return (sPL, S.val(sPL)), list(ln_group_done_tmp)

        _UNUSED = """
            group_done = []
            for gi, (g0, n, rbt, mrbt, e1, e2) in enumerate(evs):
                for c in range(DC):
                    tA = arb(OFF_TMPA[c % 4], 512)[:, 0:n]
                    w = [(sAC, e1)] + list(ln_tmp_free[c % 4])
                    vv = S.op("vector", lambda e, tA=tA, c=c, g0=g0, n=n, rbt=rbt:
                              e.tensor_mul(tA, XRs(c, g0, n), rbt),
                              waits=w, incs=[(sDV, 1)])[0]
                    pv = S.op("gpsimd", lambda e, tA=tA, c=c, g0=g0, n=n, mrbt=mrbt:
                              e.tensor_sub(XNs(c, g0, n), tA, mrbt),
                              waits=[(sDV, vv), (sAC, e2)], incs=[(sPL, 1)])[0]
                    ln_tmp_free[c % 4] = [(sPL, pv)]
                ln_bt_free[gi] = [(sPL, S.val(sPL))]
                group_done.append((sPL, S.val(sPL)))
            return (sPL, S.val(sPL)), group_done
        """

        def emit_router(group_gates):
            for gi, (g0, n) in enumerate(W3G):
                for c in range(DC):
                    w = []
                    if c == 0:
                        w.append(group_gates[gi])
                        if gi == 0:
                            w.append((sW, m_gw))
                    if c == 0:
                        w.append((sAC, S.val(sAC)))

                    def f(e, c=c, g0=g0, n=n, first=(c == 0), last=(c == DC - 1)):
                        return e.matmul(PS[6].ap()[0:8, 0:n],
                                        sb_gwT.ap()[:, c * 8:(c + 1) * 8],
                                        XNs(c, g0, n), start=first, stop=last)
                    S.op("tensor", f, waits=w, incs=[(sPE, 1)] if c == DC - 1 else [])
                m_lg = S.val(sPE)
                lg = arfp(OFF_LG, 8, 512)[:, 0:n]
                e_lg = S.op("scalar", lambda e, lg=lg, n=n:
                            e.activation(lg, PS[6].ap()[0:8, 0:n], AF.Copy),
                            waits=[(sPE, m_lg)], incs=[(sAC, 1)])[0]
                prev_exp = 0
                ntile = (n + 127) // 128
                for t in range(ntile):
                    tn = min(128, n - t * 128)
                    w = [(sAC, e_lg)]
                    if prev_exp:
                        w.append((sAC, prev_exp))

                    def t1(e, lg=lg, t=t, tn=tn):
                        return e.transpose(PS[5].ap()[0:tn, 0:8],
                                           lg[:, t * 128: t * 128 + tn],
                                           sb_idf.ap()[0:8, 0:8])
                    t1v = S.op("tensor", t1, waits=w, incs=[(sPE, 1)])[0]
                    nm = arfp(OFF_RCOLS, 128, 8)[0:tn, 0:1]
                    rs = arfp(OFF_RCOLS, 128, 8)[0:tn, 1:2]
                    rc2 = arfp(OFF_RCOLS, 128, 8)[0:tn, 2:3]
                    rv = S.op("vector", lambda e, nm=nm, tn=tn:
                              e.reduce_max(nm, PS[5].ap()[0:tn, 0:8], axis=AX.X,
                                           negate=True),
                              waits=[(sPE, t1v)], incs=[(sDV, 1)])[0]
                    pbuf = arbp(OFF_PBUF[t % 2], 128, 8)[0:tn, :]
                    av = S.op("scalar", lambda e, pbuf=pbuf, nm=nm, rs=rs, tn=tn:
                              e.activation(pbuf, PS[5].ap()[0:tn, 0:8], AF.Exp,
                                           bias=nm, accum_out=rs),
                              waits=[(sDV, rv), (sPE, S.val(sPE))], incs=[(sAC, 1)])[0]
                    prev_exp = av
                    rc = S.op("vector", lambda e, rs=rs, rc2=rc2:
                              e.reciprocal(rc2, rs),
                              waits=[(sAC, av)], incs=[(sDV, 1)])[0]
                    pm = S.op("scalar", lambda e, pbuf=pbuf, rc2=rc2:
                              e.activation(pbuf, pbuf, AF.Identity, scale=rc2),
                              waits=[(sDV, rc)], incs=[(sAC, 1)])[0]
                    t2v = S.op("tensor", lambda e, pbuf=pbuf, tn=tn:
                               e.transpose(pb_bf[0:8, 0:tn], pbuf,
                                           sb_id.ap()[0:tn, 0:tn]),
                               waits=[(sAC, pm)], incs=[(sPE, 1)])[0]
                    S.op("scalar", lambda e, g0=g0, t=t, tn=tn:
                         e.activation(ROUTa[0:8, g0 + t * 128: g0 + t * 128 + tn],
                                      pb_bf[0:8, 0:tn], AF.Copy),
                         waits=[(sPE, t2v)], incs=[(sAC, 1)])[0]
            return (sAC, S.val(sAC))

        def emit_xin(entry_waits):
            bank_free = {4: [], 5: []}
            for gi, (g0, n) in enumerate(W3G):
                for c in range(DC):
                    bk = 4 + (c % 2)
                    w = list(bank_free[bk])
                    if gi == 0 and c == 0:
                        w += list(entry_waits)

                    def f(e, c=c, g0=g0, n=n, bk=bk):
                        return e.matmul(PS[bk].ap()[:, 0:n],
                                        icsb[0:8, c * 128:(c + 1) * 128],
                                        ROUTa[0:8, g0:g0 + n], start=True, stop=True)
                    mv = S.op("tensor", f, waits=w, incs=[(sPE, 1)])[0]
                    dv = S.op("vector", lambda e, c=c, g0=g0, n=n, bk=bk:
                              e.tensor_mul(XNs(c, g0, n), XNs(c, g0, n),
                                           PS[bk].ap()[:, 0:n]),
                              waits=[(sPE, mv)], incs=[(sDV, 1)])[0]
                    bank_free[bk] = [(sDV, dv)]
            return (sDV, S.val(sDV))

        def emit_ffn_main(first, groups, entry_waits, prefetch_free=None,
                          ocs_load=None, ocs_gate=None):
            ng = len(groups)
            base = snap()
            init = list(prefetch_free) if prefetch_free is not None else []
            wbtb_free = [list(init) for _ in range(6)]
            bank_free = {i: list(base) for i in range(6)}
            osh_free = [[], []]
            tsb_free = [[], []]
            t2_free = [[], []]
            ps6_free = list(base)
            ew = list(entry_waits)
            wbt_ready = {}
            chunk_done = [None] * DC

            def emit_weights(o):
                rdy = []
                for hf in range(2):
                    slot = (o * 2 + hf) % 6
                    wbtb = arb(OFF_WBTB[slot], 2048)
                    if first:
                        def fd(e, o=o, hf=hf, wbtb=wbtb):
                            srcp = fwT[hf * 2048:(hf + 1) * 2048, o * 128:(o + 1) * 128]
                            return e.dma_start(
                                out=wbtb.rearrange("p (c o) -> p c o", c=16),
                                in_=srcp.rearrange("(c p) o -> p c o", p=128))
                        dv_ = S.op("gpsimd", fd, waits=list(wbtb_free[slot]),
                                   incs=[(sW, 16)])[0]
                        sv = S.op("vector", lambda e, wbtb=wbtb:
                                  e.tensor_scalar(wbtb.bitcast(I16), wbtb.bitcast(I16),
                                                  -32768, 0x3F80,
                                                  ALU.bitwise_and, ALU.bitwise_or),
                                  waits=[(sW, dv_)], incs=[(sDV, 1)])[0]
                        hv = S.op("sync", lambda e, o=o, hf=hf, wbtb=wbtb:
                                  e.dma_start(
                                      out=scr_fw[(o * 2 + hf) * 128:(o * 2 + hf + 1) * 128, :],
                                      in_=wbtb),
                                  waits=[(sDV, sv)], incs=[(sHW, 16)])[0]
                        rdy.append((sDV, sv))
                        wbtb_free[slot] = [(sHW, hv)]
                    else:
                        hv = S.op("sync", lambda e, o=o, hf=hf, wbtb=wbtb:
                                  e.dma_start(
                                      out=wbtb,
                                      in_=scr_fw[(o * 2 + hf) * 128:(o * 2 + hf + 1) * 128, :]),
                                  waits=list(wbtb_free[slot]), incs=[(sHW, 16)])[0]
                        rdy.append((sHW, hv))
                        wbtb_free[slot] = []
                wbt_ready[o] = rdy

            if PREFETCH_W:
                emit_weights(0)
                if PF_DIST >= 2:
                    emit_weights(1)
            if ocs_load is not None:
                ocs_load()
            for o in range(DC):
                if not PREFETCH_W:
                    emit_weights(o)
                last_mm = {}
                for hf in range(2):
                    slot = (o * 2 + hf) % 6
                    wbtb = arb(OFF_WBTB[slot], 2048)
                    for gi, (g0, n) in enumerate(groups):
                        bk = (o % 2) * 3 + gi
                        for kk in range(16):
                            k = hf * 16 + kk
                            w = []
                            if kk == 0:
                                w.append(wbt_ready[o][hf])
                                if hf == 0:
                                    w += list(bank_free[bk])
                                    if o == 0 and gi == 0:
                                        w += ew

                            def fm(e, kk=kk, k=k, g0=g0, n=n, bk=bk, wbtb=wbtb):
                                return e.matmul(PS[bk].ap()[:, 0:n],
                                                wbtb[:, kk * 128:(kk + 1) * 128],
                                                XNs(k, g0, n),
                                                start=(k == 0), stop=(k == 31))
                            incs = [(sPE, 1)] if (hf == 1 and kk == 15) else []
                            r = S.op("tensor", fm, waits=w, incs=incs)
                            if incs:
                                last_mm[gi] = r[0]
                for hf in range(2):
                    wbtb_free[(o * 2 + hf) % 6].append((sPE, last_mm[ng - 1]))
                if PREFETCH_W and PF_EARLY and o + PF_DIST < DC:
                    emit_weights(o + PF_DIST)
                for gi, (g0, n) in enumerate(groups):
                    bk = (o % 2) * 3 + gi

                    def fo(e, o=o, g0=g0, n=n):
                        return e.matmul(PS[6].ap()[:, 0:n],
                                        icsb[0:8, o * 128:(o + 1) * 128],
                                        ROUTa[0:8, g0:g0 + n], start=True, stop=True)
                    wfo = list(ps6_free)
                    if o == 0 and gi == 0 and ocs_gate is not None:
                        wfo += list(ocs_gate)
                    ov = S.op("tensor", fo, waits=wfo, incs=[(sPE, 1)])[0]
                    osh = arf(OFF_OSH[gi % 2], 512)[:, 0:n]
                    av = S.op("scalar", lambda e, osh=osh, n=n:
                              e.activation(osh, PS[6].ap()[:, 0:n], AF.Copy, scale=0.5),
                              waits=[(sPE, ov)] + list(osh_free[gi % 2]),
                              incs=[(sAC, 1)])[0]
                    ps6_free = [(sAC, av)]
                    tsb = arf(OFF_TSB[gi % 2], 512)[:, 0:n]
                    tv = S.op("vector", lambda e, tsb=tsb, osh=osh, n=n, bk=bk:
                              e.tensor_mul(tsb, PS[bk].ap()[:, 0:n], osh),
                              waits=[(sPE, last_mm[gi]), (sAC, av)] + list(tsb_free[gi % 2]),
                              incs=[(sDV, 1)])[0]
                    bank_free[bk] = [(sDV, tv)]
                    osh_free[gi % 2] = [(sDV, tv)]
                    t2 = arf(OFF_T2[gi % 2], 512)[:, 0:n]
                    a2 = S.op("scalar", lambda e, t2=t2, tsb=tsb, o=o:
                              e.activation(t2, tsb, AF.Identity, bias=sb_fbh.ap()[:, o:o + 1]),
                              waits=[(sDV, tv)] + list(t2_free[gi % 2]),
                              incs=[(sAC, 1)])[0]
                    tsb_free[gi % 2] = [(sAC, a2)]
                    pv = S.op("vector", lambda e, t2=t2, o=o, g0=g0, n=n:
                              e.tensor_add(XRs(o, g0, n), XRs(o, g0, n), t2),
                              waits=[(sAC, a2)], incs=[(sDV, 1)])[0]
                    t2_free[gi % 2] = [(sDV, pv)]
                    chunk_done[o] = (sDV, pv)
                if PREFETCH_W and not PF_EARLY and o + PF_DIST < DC:
                    emit_weights(o + PF_DIST)
            return (sDV, S.val(sDV)), chunk_done

        m_ics = S.op("gpsimd", lambda e: e.dma_start(out=icsb, in_=ics),
                     incs=[(sW, 16)])[0]

        tap_src = [None]
        tap_m = [None]
        if tap == 'load':
            tap_src[0] = 'XR'
            tap_m[0] = xr_ready[DC - 1]
        if tap_m[0] is None:
            m_ln1, lng1 = emit_ln(hmask=False, entry_waits=m_const,
                                  xr_chunk_ready=xr_ready)
            if tap == 'ln1':
                tap_src[0] = 'XN'
                tap_m[0] = m_ln1
        if tap_m[0] is None:
            m_rt1 = emit_router(lng1)
            m_xin1 = emit_xin(entry_waits=[m_rt1, (sW, m_ics)])
            if tap == 'xin1':
                tap_src[0] = 'XN'
                tap_m[0] = m_xin1
        if tap_m[0] is None:
            xin_pe = (sPE, S.val(sPE))
            ocs_ready = []

            def load_ocs():
                v = S.op("gpsimd", lambda e: e.dma_start(out=icsb, in_=ocs),
                         waits=[xin_pe], incs=[(sW, 16)])[0]
                ocs_ready.append((sW, v))
            m_ffn1, ffn1_cd = emit_ffn_main(first=True, groups=W3G,
                                            entry_waits=[m_xin1],
                                            prefetch_free=[tm_free[0][0], tm_free[1][0]],
                                            ocs_load=load_ocs,
                                            ocs_gate=ocs_ready)
        else:
            upto = 0
            m_ffn1 = tap_m[0]

        if upto >= 2:
            m_ln2, _lng2 = emit_ln(hmask=True, entry_waits=[],
                                   xr_chunk_ready=ffn1_cd)
            if tap == 'ln2':
                tap_src[0] = 'XN'
                tap_m[0] = m_ln2

            skip2 = tap_m[0] is not None
            cw_free = [[m_rt1], [m_rt1], [m_rt1], [m_rt1]]
            base = snap()
            bank_free_c = {i: list(base) for i in range(8)}
            uslot = 0
            for s3 in range(3 if not skip2 else 0):
                for qt in range(4):
                    mmcount = {}
                    lastv = {}
                    unit_last = {}
                    for u in range(24):
                        cvt = u // 8
                        k4 = u % 8
                        slot = uslot % 4
                        uslot += 1
                        cw = arb(OFF_CWR[slot], 1024)

                        def fd(e, cvt=cvt, k4=k4, s3=s3, qt=qt, cw=cw):
                            src = conv_w[cvt, k4 * 512:(k4 + 1) * 512,
                                         s3 * 1024 + qt * 256: s3 * 1024 + qt * 256 + 256]
                            return e.dma_start(
                                out=cw.rearrange("p (c o) -> p c o", c=4),
                                in_=src.rearrange("(c p) o -> p c o", p=128))
                        dv_ = S.op("gpsimd", fd, waits=list(cw_free[slot]),
                                   incs=[(sW, 16)])[0]
                        nmm = 0
                        for kk in range(4):
                            k = k4 * 4 + kk
                            for oc2 in range(2):
                                for gi, (g0, n) in enumerate(CG):
                                    bk = (qt % 2) * 4 + oc2 * 2 + gi
                                    key = (oc2, gi)
                                    cnt = mmcount.get(key, 0)
                                    mmcount[key] = cnt + 1
                                    nmm += 1
                                    w = []
                                    if kk == 0 and oc2 == 0 and gi == 0:
                                        w.append((sW, dv_))
                                    if cnt == 0:
                                        w += list(bank_free_c[bk])
                                        if s3 == 0 and qt == 0:
                                            w.append(m_ln2)
                                    last_in_unit = (nmm == 16)

                                    def fm(e, cw=cw, kk=kk, oc2=oc2, g0=g0, n=n,
                                           cvt=cvt, k=k, bk=bk,
                                           first=(cnt == 0), last=(cnt == 95)):
                                        return e.matmul(
                                            PS[bk].ap()[:, 0:n],
                                            cw[:, kk * 256 + oc2 * 128:
                                               kk * 256 + oc2 * 128 + 128],
                                            XNs(k, g0 + cvt, n),
                                            start=first, stop=last)
                                    incs = []
                                    if cnt == 95:
                                        incs.append((sPE, 1))
                                    elif last_in_unit:
                                        incs.append((sPE, 1))
                                    r = S.op("tensor", fm, waits=w, incs=incs)
                                    if cnt == 95:
                                        lastv[key] = r[0]
                                    if last_in_unit:
                                        unit_last[slot] = r[0]
                        cw_free[slot] = [(sPE, unit_last[slot])]
                    for oc2 in range(2):
                        for gi, (g0, n) in enumerate(CG):
                            bk = (qt % 2) * 4 + oc2 * 2 + gi
                            ocg = s3 * 8 + qt * 2 + oc2

                            def fe(e, bk=bk, ocg=ocg, g0=g0, n=n):
                                return e.activation(QKVs(ocg, g0, n),
                                                    PS[bk].ap()[:, 0:n], AF.Identity,
                                                    bias=sb_convb.ap()[:, ocg:ocg + 1])
                            ev_ = S.op("scalar", fe, waits=[(sPE, lastv[(oc2, gi)])],
                                       incs=[(sAC, 1)])[0]
                            bank_free_c[bk] = [(sAC, ev_)]
            m_conv = (sAC, S.val(sAC))

            pw_free = [list(cw_free[i]) for i in range(4)]
            base = snap()
            bank_free_p = {i: list(base) for i in range(4)}
            pslot = 0
            hp_list = [] if skip2 else [(wq, sb_bq), (wk, sb_bk), (wv, sb_bv)]
            for pi, (wsrc_d, bias_sb) in enumerate(hp_list):
                for j in range(8):
                    slot = pslot % 4
                    pslot += 1
                    cw = arb(OFF_CWR[slot], 1024)

                    def fd(e, wsrc_d=wsrc_d, j=j, cw=cw):
                        return e.dma_start(
                            out=cw.rearrange("p (c o) -> p c o", c=8),
                            in_=wsrc_d[:, j * 128:(j + 1) * 128]
                            .rearrange("(c p) o -> p c o", p=128))
                    dv_ = S.op("gpsimd", fd, waits=list(pw_free[slot]),
                               incs=[(sW, 16)])[0]
                    lastg = {}
                    for gi, (g0, n) in enumerate(CG):
                        bk = (j % 2) * 2 + gi
                        for k in range(8):
                            w = []
                            if k == 0:
                                w.append((sW, dv_))
                                w += list(bank_free_p[bk])
                                if pi == 0 and j == 0 and gi == 0:
                                    w.append(m_conv)

                            def fm(e, cw=cw, k=k, g0=g0, n=n, bk=bk, pi=pi):
                                return e.matmul(PS[bk].ap()[:, 0:n],
                                                cw[:, k * 128:(k + 1) * 128],
                                                QKVs(pi * 8 + k, g0, n),
                                                start=(k == 0), stop=(k == 7))
                            incs = [(sPE, 1)] if k == 7 else []
                            r = S.op("tensor", fm, waits=w, incs=incs)
                            if incs:
                                lastg[gi] = r[0]
                    pw_free[slot] = [(sPE, lastg[1])]
                    for gi, (g0, n) in enumerate(CG):
                        bk = (j % 2) * 2 + gi

                        def fe(e, bk=bk, dstc=pi * 8 + j, g0=g0, n=n,
                               bias_sb=bias_sb, j=j, pi=pi):
                            return e.activation(XNs(dstc, g0, n), PS[bk].ap()[:, 0:n],
                                                AF.Identity, bias=bias_sb.ap()[:, j:j + 1],
                                                scale=(QSCALE if pi == 0 else 1.0))
                        ev_ = S.op("scalar", fe, waits=[(sPE, lastg[gi])],
                                   incs=[(sAC, 1)])[0]
                        bank_free_p[bk] = [(sAC, ev_)]
            m_hproj = (sAC, S.val(sAC))
            if tap == 'hproj':
                tap_src[0] = 'XN0'
                tap_m[0] = m_hproj
            skip2 = tap_m[0] is not None

            base = snap()
            sc_bank_free = {i: list(base) for i in range(4)}
            o_bank_free = {4: list(base), 5: list(base)}
            pb_half_free = [list(base), list(base)]
            pn_free = [[], [], [], []]
            acol = arfp(OFF_ACOL, 128, 64)
            vhT = arb(OFF_VHT, 1024)
            pts = arb(OFF_PTS, 2048)
            for b in range(NB if not skip2 else 0):
                cb = b * BL
                for c8 in range(8):
                    w = list(pb_half_free[0]) + list(pb_half_free[1]) if c8 == 0 else []
                    if b == 0 and c8 == 0:
                        w.append(m_hproj)

                    def ft(e, c8=c8, cb=cb):
                        return e.transpose(
                            pb_bf[:, c8 * 128:(c8 + 1) * 128],
                            XNa[:, (16 + c8) * TH + cb:(16 + c8) * TH + cb + BL],
                            sb_id.ap())
                    r = S.op("tensor", ft, waits=w, incs=[(sPE, 1)] if c8 == 7 else [])
                    if c8 == 7:
                        vT_done = r[0]
                vh_ev = S.op("scalar", lambda e, vhT=vhT:
                             e.activation(vhT, pb_bf[:, :], AF.Copy),
                             waits=[(sPE, vT_done)], incs=[(sAC, 1)])[0]
                pb_half_free[0] = [(sAC, vh_ev)]
                pb_half_free[1] = [(sAC, vh_ev)]
                pts_ev = []
                for bkq in range(4):
                    lastmm = None
                    for idx in range(4):
                        h = bkq + idx * 4
                        w = list(sc_bank_free[bkq]) if idx == 0 else []

                        def fm(e, h=h, idx=idx, cb=cb, bkq=bkq):
                            pbase = 64 * (h % 2)
                            qh = XNa[pbase:pbase + 64,
                                     (h // 2) * TH + cb:(h // 2) * TH + cb + BL]
                            kh = XNa[pbase:pbase + 64,
                                     (8 + h // 2) * TH + cb:(8 + h // 2) * TH + cb + BL]
                            return e.matmul(PS[bkq].ap()[:, idx * 128:(idx + 1) * 128],
                                            qh, kh, start=True, stop=True)
                        r = S.op("tensor", fm, waits=w,
                                 incs=[(sPE, 1)] if idx == 3 else [])
                        if idx == 3:
                            lastmm = r[0]
                    exp_last = 0
                    pn_ready = {}
                    for idx in range(4):
                        h = bkq + idx * 4
                        cslot = idx + 4 * (bkq % 2)
                        nm = acol[:, cslot: cslot + 1]
                        rs = acol[:, 8 + cslot: 9 + cslot]
                        rc2 = acol[:, 16 + cslot: 17 + cslot]
                        rv = S.op("vector", lambda e, nm=nm, bkq=bkq, idx=idx:
                                  e.reduce_max(nm,
                                               PS[bkq].ap()[:, idx * 128:(idx + 1) * 128],
                                               axis=AX.X, negate=True),
                                  waits=[(sPE, lastmm)], incs=[(sDV, 1)])[0]
                        pn = arb(OFF_PN[idx], 128)
                        av = S.op("scalar", lambda e, pn=pn, nm=nm, bkq=bkq, idx=idx:
                                  e.activation(pn,
                                               PS[bkq].ap()[:, idx * 128:(idx + 1) * 128],
                                               AF.Exp, bias=nm),
                                  waits=[(sDV, rv)] + list(pn_free[idx]),
                                  incs=[(sAC, 1)])[0]
                        exp_last = av
                        pv = S.op("vector", lambda e, pn=pn, rs=rs:
                                  e.scalar_tensor_tensor(pn, pn, 1.0, sb_mask.ap(),
                                                         ALU.mult, ALU.mult,
                                                         accum_out=rs),
                                  waits=[(sAC, av)], incs=[(sDV, 1)])[0]
                        rc = S.op("vector", lambda e, rs=rs, rc2=rc2:
                                  e.reciprocal(rc2, rs),
                                  waits=[(sDV, pv)], incs=[(sDV, 1)])[0]
                        pnv = S.op("scalar", lambda e, pn=pn, rc2=rc2:
                                   e.activation(pn, pn, AF.Identity, scale=rc2),
                                   waits=[(sDV, rc)], incs=[(sAC, 1)])[0]
                        pn_ready[idx] = pnv
                    sc_bank_free[bkq] = [(sAC, exp_last)]
                    half = bkq % 2
                    pbv = pb_bf if half == 0 else pb_bf6
                    ptlast = None
                    for idx in range(4):
                        h = bkq + idx * 4
                        pn = arb(OFF_PN[idx], 128)
                        w = [(sAC, pn_ready[idx])]
                        if idx == 0:
                            w += list(pb_half_free[half])

                        def ft(e, pn=pn, idx=idx, pbv=pbv):
                            return e.transpose(
                                pbv[:, idx * 128:(idx + 1) * 128],
                                pn, sb_id.ap())
                        r = S.op("tensor", ft, waits=w, incs=[(sPE, 1)])
                        pn_free[idx] = [(sPE, r[0])]
                        ptlast = r[0]
                    ev_ = S.op("scalar", lambda e, bkq=bkq, pbv=pbv:
                               e.activation(pts[:, bkq * 512:(bkq + 1) * 512],
                                            pbv[:, 0:512],
                                            AF.Copy),
                               waits=[(sPE, ptlast)], incs=[(sAC, 1)])[0]
                    pb_half_free[half] = [(sAC, ev_)]
                    pts_ev.append(ev_)
                o_last = {4: None, 5: None}
                for hh in range(H):
                    c = hh // 2
                    bk = 4 + (c % 2)
                    w = []
                    if hh == 0:
                        w += list(o_bank_free[4]) + list(o_bank_free[5])
                        w.append((sAC, pts_ev[3]))
                        w.append((sAC, vh_ev))

                    def fm(e, hh=hh, c=c, bk=bk):
                        pcol = (hh % 4) * 512 + (hh // 4) * 128
                        return e.matmul(
                            PS[bk].ap()[64 * (hh % 2):64 * (hh % 2) + 64,
                                        (c // 2) * 128:(c // 2) * 128 + 128],
                            vhT[:, hh * 64:(hh + 1) * 64],
                            pts[:, pcol: pcol + 128],
                            start=True, stop=True)
                    is_last = hh in (13, 15)
                    r = S.op("tensor", fm, waits=w,
                             incs=[(sPE, 1)] if is_last else [])
                    if is_last:
                        o_last[bk] = r[0]
                for c in range(8):
                    bk = 4 + (c % 2)

                    def fe(e, c=c, bk=bk, cb=cb):
                        return e.activation(
                            XNa[:, c * TH + cb: c * TH + cb + BL],
                            PS[bk].ap()[:, (c // 2) * 128:(c // 2) * 128 + 128],
                            AF.Copy)
                    ev_ = S.op("scalar", fe, waits=[(sPE, o_last[bk])],
                               incs=[(sAC, 1)])[0]
                    o_bank_free[bk] = [(sAC, ev_)]
            m_attn = (sAC, S.val(sAC))
            if tap == 'attn':
                tap_src[0] = 'XN0'
                tap_m[0] = m_attn
            skip2 = tap_m[0] is not None

            base = snap()
            bank_free_w = {i: list(base) for i in range(4)}
            for j in range(8 if not skip2 else 0):
                slot = pslot % 4
                pslot += 1
                cw = arb(OFF_CWR[slot], 1024)

                def fd(e, j=j, cw=cw):
                    return e.dma_start(
                        out=cw.rearrange("p (c o) -> p c o", c=8),
                        in_=wo[:, j * 128:(j + 1) * 128]
                        .rearrange("(c p) o -> p c o", p=128))
                dv_ = S.op("gpsimd", fd, waits=list(pw_free[slot]), incs=[(sW, 16)])[0]
                lastg = {}
                for gi, (g0, n) in enumerate(CG):
                    bk = (j % 2) * 2 + gi
                    for k in range(8):
                        w = []
                        if k == 0:
                            w.append((sW, dv_))
                            w += list(bank_free_w[bk])
                            if j == 0 and gi == 0:
                                w.append(m_attn)

                        def fm(e, cw=cw, k=k, g0=g0, n=n, bk=bk):
                            return e.matmul(PS[bk].ap()[:, 0:n],
                                            cw[:, k * 128:(k + 1) * 128],
                                            XNs(k, g0, n), start=(k == 0), stop=(k == 7))
                        incs = [(sPE, 1)] if k == 7 else []
                        r = S.op("tensor", fm, waits=w, incs=incs)
                        if incs:
                            lastg[gi] = r[0]
                pw_free[slot] = [(sPE, lastg[1])]
                for gi, (g0, n) in enumerate(CG):
                    bk = (j % 2) * 2 + gi

                    def fe(e, bk=bk, j=j, g0=g0, n=n):
                        return e.activation(QKVs(j, g0, n), PS[bk].ap()[:, 0:n],
                                            AF.Identity, bias=sb_bo.ap()[:, j:j + 1])
                    ev_ = S.op("scalar", fe, waits=[(sPE, lastg[gi])],
                               incs=[(sAC, 1)])[0]
                    bank_free_w[bk] = [(sAC, ev_)]
            m_wo = (sAC, S.val(sAC))

            base = snap()
            bank_free_j = {i: list(base) for i in range(4)}
            tpk_free = [list(base), list(base)]
            proj_cd = [None] * DC
            for o in range(DC if not skip2 else 0):
                slot = pslot % 4
                pslot += 1
                cw = arb(OFF_CWR[slot], 1024)

                def fd(e, o=o, cw=cw):
                    return e.dma_start(
                        out=cw.rearrange("p (c o) -> p c o", c=8),
                        in_=projw[:, o * 128:(o + 1) * 128]
                        .rearrange("(c p) o -> p c o", p=128))
                dv_ = S.op("gpsimd", fd, waits=list(pw_free[slot]), incs=[(sW, 16)])[0]
                lastg = {}
                for gi, (g0, n) in enumerate(CG):
                    bk = (o % 2) * 2 + gi
                    for k in range(8):
                        w = []
                        if k == 0:
                            w.append((sW, dv_))
                            w += list(bank_free_j[bk])
                            if o == 0 and gi == 0:
                                w.append(m_wo)

                        def fm(e, cw=cw, k=k, g0=g0, n=n, bk=bk):
                            return e.matmul(PS[bk].ap()[:, 0:n],
                                            cw[:, k * 128:(k + 1) * 128],
                                            QKVs(k, g0, n), start=(k == 0), stop=(k == 7))
                        incs = [(sPE, 1)] if k == 7 else []
                        r = S.op("tensor", fm, waits=w, incs=incs)
                        if incs:
                            lastg[gi] = r[0]
                pw_free[slot] = [(sPE, lastg[1])]
                for gi, (g0, n) in enumerate(CG):
                    bk = (o % 2) * 2 + gi
                    tpk = arf(OFF_TSB[gi % 2], 512)[:, 0:n]

                    def fe(e, bk=bk, o=o, n=n, tpk=tpk):
                        return e.activation(tpk, PS[bk].ap()[:, 0:n], AF.Identity,
                                            bias=sb_pb.ap()[:, o:o + 1])
                    a_ = S.op("scalar", fe,
                              waits=[(sPE, lastg[gi])] + list(tpk_free[gi % 2]),
                              incs=[(sAC, 1)])[0]
                    bank_free_j[bk] = [(sAC, a_)]
                    dv_ = S.op("vector", lambda e, o=o, g0=g0, n=n, tpk=tpk:
                               e.tensor_add(XRs(o, 1 + g0, n), XRs(o, 1 + g0, n), tpk),
                               waits=[(sAC, a_)], incs=[(sDV, 1)])[0]
                    tpk_free[gi % 2] = [(sDV, dv_)]
                    proj_cd[o] = (sDV, dv_)
            m_proj = (sDV, S.val(sDV))
            m_projpe = (sPE, S.val(sPE))
            del skip2
            if tap == 'proj':
                tap_src[0] = 'XR'
                tap_m[0] = m_proj

            if tap_m[0] is not None:
                m_final = tap_m[0]
            else:
                m_ics2 = S.op("gpsimd", lambda e: e.dma_start(out=icsb, in_=ics),
                              waits=[(sPE, S.val(sPE))], incs=[(sW, 16)])[0]
                m_ln3, lng3 = emit_ln(hmask=False, entry_waits=[],
                                      xr_chunk_ready=proj_cd)
                m_rt2 = emit_router(lng3)
                m_xin2 = emit_xin(entry_waits=[m_rt2, (sW, m_ics2)])
                xin2_pe = (sPE, S.val(sPE))
                ocs2_ready = []

                def load_ocs2():
                    v = S.op("gpsimd", lambda e: e.dma_start(out=icsb, in_=ocs),
                             waits=[xin2_pe], incs=[(sW, 16)])[0]
                    ocs2_ready.append((sW, v))
                m_final, _ = emit_ffn_main(first=False, groups=W2G,
                                           entry_waits=[m_xin2, m_projpe],
                                           prefetch_free=[m_projpe],
                                           ocs_load=load_ocs2, ocs_gate=ocs2_ready)
        else:
            m_final = m_ffn1

        if tap_src[0] == 'XN':
            srcfn, soff = XNs, 1
        elif tap_src[0] == 'XN0':
            srcfn, soff = XNs, 0
        else:
            srcfn, soff = XRs, 1
        ost_free = [[m_final], [m_final]]
        pb_free_o = [[m_final], [m_final]]
        for tc in range(8):
            ost = arb(OFF_OSTAGE[tc % 2], 4096)
            lastE = None
            for fcg in range(4):
                pbv = pb_bf if fcg % 2 == 0 else pb_bf6
                lastT = None
                for j in range(8):
                    fc = fcg * 8 + j
                    w = []
                    if j == 0:
                        w += list(pb_free_o[fcg % 2])
                        if fcg == 0 and tc == 0:
                            w.append(m_final)

                    def ft(e, fc=fc, j=j, tc=tc, pbv=pbv):
                        return e.transpose(pbv[:, j * 128:(j + 1) * 128],
                                           srcfn(fc, soff + tc * 128, 128),
                                           sb_id.ap())
                    r = S.op("tensor", ft, waits=w,
                             incs=[(sPE, 1)] if j == 7 else [])
                    if j == 7:
                        lastT = r[0]

                def fe(e, fcg=fcg, ost=ost, pbv=pbv):
                    return e.activation(ost[:, fcg * 1024:(fcg + 1) * 1024],
                                        pbv[:, :], AF.Copy)
                w = [(sPE, lastT)] + list(ost_free[tc % 2] if fcg == 0 else [])
                ev_ = S.op("scalar", fe, waits=w, incs=[(sAC, 1)])[0]
                pb_free_o[fcg % 2] = [(sAC, ev_)]
                lastE = ev_
            ov = S.op("gpsimd", lambda e, tc=tc, ost=ost:
                      e.dma_start(out=x_out[tc * 128:(tc + 1) * 128, :], in_=ost),
                      waits=[(sAC, lastE)], incs=[(sOUT, 16)])[0]
            ost_free[tc % 2] = [(sOUT, ov)]
        m_out = S.val(sOUT)
        S.op("gpsimd", lambda e: e.nop(), waits=[(sOUT, m_out)])
        S.op("sync", lambda e: e.nop(), waits=[(sOUT, m_out)])

        S.check({id(sW): 'sW', id(sHW): 'sHW', id(sPE): 'sPE', id(sDV): 'sDV',
                 id(sAC): 'sAC', id(sPL): 'sPL', id(sOUT): 'sOUT'})

        with nc.Block() as block:
            @block.sync
            def _(e):
                for fn in S.ev["sync"]:
                    fn(e)

            @block.tensor
            def _(e):
                for fn in S.ev["tensor"]:
                    fn(e)

            @block.vector
            def _(e):
                for fn in S.ev["vector"]:
                    fn(e)

            @block.scalar
            def _(e):
                for fn in S.ev["scalar"]:
                    fn(e)

            @block.gpsimd
            def _(e):
                for fn in S.ev["gpsimd"]:
                    fn(e)

    return nc


_NC_CACHE = {}


def _get_nc(upto=99, tap=None):
    key = (upto, tap)
    if key not in _NC_CACHE:
        _NC_CACHE[key] = _build_program(upto, tap)
    return _NC_CACHE[key]


def _host_inputs(L0_attn_inputs, L0_attn_conv_w, L0_attn_conv_b, L0_attn_wq, L0_attn_bq,
                 L0_attn_wk, L0_attn_bk, L0_attn_wv, L0_attn_bv, L0_attn_wo, L0_attn_bo,
                 L0_attn_proj_w, L0_attn_proj_b, L0_ffn_weight, L0_ffn_bias, L0_ffn_gate_w,
                 L0_ffn_in_channel_scale, L0_ffn_out_channel_scale):
    f32 = np.float32
    x = np.asarray(L0_attn_inputs, f32).reshape(8192, D)
    shared = dict(
        conv_w=np.ascontiguousarray(np.asarray(L0_attn_conv_w, f32)),
        convb_c=np.ascontiguousarray(np.asarray(L0_attn_conv_b, f32).reshape(24, 128).T),
        wq=np.ascontiguousarray(np.asarray(L0_attn_wq, f32).reshape(WD, WD)),
        wk=np.ascontiguousarray(np.asarray(L0_attn_wk, f32).reshape(WD, WD)),
        wv=np.ascontiguousarray(np.asarray(L0_attn_wv, f32).reshape(WD, WD)),
        wo=np.ascontiguousarray(np.asarray(L0_attn_wo, f32).reshape(WD, WD)),
        bq_c=np.ascontiguousarray(np.asarray(L0_attn_bq, f32).reshape(8, 128).T),
        bk_c=np.ascontiguousarray(np.asarray(L0_attn_bk, f32).reshape(8, 128).T),
        bv_c=np.ascontiguousarray(np.asarray(L0_attn_bv, f32).reshape(8, 128).T),
        bo_c=np.ascontiguousarray(np.asarray(L0_attn_bo, f32).reshape(8, 128).T),
        projw=np.ascontiguousarray(np.asarray(L0_attn_proj_w, f32)),
        projb_c=np.ascontiguousarray(np.asarray(L0_attn_proj_b, f32).reshape(32, 128).T),
        fwT=np.ascontiguousarray(np.asarray(L0_ffn_weight, f32).T),
        fb_c=np.ascontiguousarray(np.asarray(L0_ffn_bias, f32).reshape(32, 128).T),
        gwT3=np.ascontiguousarray(
            np.asarray(L0_ffn_gate_w, f32).T.reshape(DC, 128, 8)
            .transpose(1, 0, 2).reshape(128, 256)),
        ics=np.ascontiguousarray(np.asarray(L0_ffn_in_channel_scale, f32)),
        ocs=np.ascontiguousarray(np.asarray(L0_ffn_out_channel_scale, f32)),
    )
    in_maps = []
    for i in range(NCORES):
        t0 = i * T
        xs = np.zeros((TH, D), f32)
        lo = max(0, t0 - 1)
        hi = min(8192, t0 + T + 1)
        xs[1 + (lo - t0): 1 + (hi - t0), :] = x[lo:hi, :]
        hm = np.ones((1, TH), f32)
        if i == 0:
            hm[0, 0] = 0.0
        if i == NCORES - 1:
            hm[0, TH - 1] = 0.0
        in_maps.append(dict(x_in=xs, hmrow=hm, **shared))
    return in_maps


def kernel(**inputs):
    upto = inputs.pop("_upto", 99)
    tap = inputs.pop("_tap", None)
    in_maps = _host_inputs(**inputs)
    nc = _get_nc(upto, tap)
    res = run_bass_kernel_spmd(nc, in_maps, core_ids=list(range(NCORES)))
    out = np.concatenate([np.asarray(r["x_out"], np.float32) for r in res.results],
                         axis=0)
    return out.reshape(1, 8192, D)


# version 10
# speedup vs baseline: 1.0023x; 1.0023x over previous
import numpy as np
import ml_dtypes
import contextlib
import concourse.bass as bass
import concourse.mybir as mybir
from concourse.bass_utils import run_bass_kernel_spmd

F32 = mybir.dt.float32
BF16 = mybir.dt.bfloat16
I16 = mybir.dt.int16
AF = mybir.ActivationFunctionType
ALU = mybir.AluOpType
AX = mybir.AxisListType
BFNP = ml_dtypes.bfloat16

D = 4096
DC = 32
WD = 1024
H = 16
BL = 128
T = 1024
TH = 1026
NB = 8
NCORES = 8
EPS = 1e-5
INV_D = 1.0 / D
QSCALE = 0.125
PIPE_LN = True
PREFETCH_W = True
PF_EARLY = False
PF_DIST = 2

W3G = [(0, 512), (512, 512), (1024, 2)]
W2G = [(1, 512), (513, 512)]
CG = [(0, 512), (512, 512)]

OFF_WBTB = [0, 4096, 8192, 12288, 16384, 20480]
OFF_ICS = 24576
OFF_MROW = 32768
OFF_VROW = 34816
OFF_RROW = 36864
OFF_MRROW = 38912
OFF_RBFROW = 40960
OFF_MRBFROW = 41984
OFF_RBT = [43008, 44032, 45056]
OFF_MRBT = [46080, 47104, 48128]
OFF_SQ = [57344, 58368, 59392, 60416]
OFF_TMPA = [61440, 62464, 63488, 64512]
OFF_OSH = [32768, 34816]
OFF_TSB = [36864, 38912]
OFF_T2 = [45056, 47104]
OFF_LG = 49152
OFF_PBUF = [51200, 51456]
OFF_RCOLS = 51712
OFF_QKV = 0
OFF_CWR = [49152 + i * 2048 for i in range(4)]
OFF_VHT = 57344
OFF_PTS = 59392
OFF_PN = [63488 + i * 256 for i in range(4)]
OFF_ACOL = 64512
OFF_OSTAGE = [49152, 57344]


class Sched:
    def __init__(self):
        self.ev = {k: [] for k in ("sync", "tensor", "vector", "scalar", "gpsimd")}
        self.cnt = {}
        self.audit = []
        self.producer = {}

    def val(self, sem):
        return self.cnt.get(id(sem), 0)

    def bump(self, sem, by):
        self.cnt[id(sem)] = self.cnt.get(id(sem), 0) + by
        return self.cnt[id(sem)]

    def op(self, eng, fn, waits=(), incs=()):
        wl = tuple((s, v) for (s, v) in waits if v > 0)
        for (s, v) in wl:
            self.audit.append((eng, id(s), v, len(self.ev[eng])))
        after = [self.bump(s, by) for (s, by) in incs]
        il = tuple(incs)
        for (s, by) in il:
            self.producer.setdefault(id(s), eng)

        def run(e, fn=fn, wl=wl, il=il):
            for (s, v) in wl:
                e.wait_ge(s, v)
            inst = fn(e)
            for (s, by) in il:
                inst = inst.then_inc(s, by)

        self.ev[eng].append(run)
        return after

    def check(self, names):
        bad = 0
        for (eng, sid, v, pos) in self.audit:
            if v > self.cnt.get(sid, 0):
                print(f"AUDIT: {eng} event#{pos} waits {names.get(sid,'?')}>={v} "
                      f"but final={self.cnt.get(sid,0)}")
                bad += 1
        if bad:
            raise AssertionError(f"{bad} unreachable waits")


def _build_program(upto=99, tap=None):
    nc = bass.Bass("TRN2", target_bir_lowering=False)

    x_in = nc.dram_tensor("x_in", [TH, D], F32, kind="ExternalInput").ap()
    hmrow = nc.dram_tensor("hmrow", [1, TH], F32, kind="ExternalInput").ap()
    conv_w = nc.dram_tensor("conv_w", [3, D, 3072], F32, kind="ExternalInput").ap()
    convb_c = nc.dram_tensor("convb_c", [128, 24], F32, kind="ExternalInput").ap()
    wq = nc.dram_tensor("wq", [WD, WD], F32, kind="ExternalInput").ap()
    wk = nc.dram_tensor("wk", [WD, WD], F32, kind="ExternalInput").ap()
    wv = nc.dram_tensor("wv", [WD, WD], F32, kind="ExternalInput").ap()
    wo = nc.dram_tensor("wo", [WD, WD], F32, kind="ExternalInput").ap()
    bq_c = nc.dram_tensor("bq_c", [128, 8], F32, kind="ExternalInput").ap()
    bk_c = nc.dram_tensor("bk_c", [128, 8], F32, kind="ExternalInput").ap()
    bv_c = nc.dram_tensor("bv_c", [128, 8], F32, kind="ExternalInput").ap()
    bo_c = nc.dram_tensor("bo_c", [128, 8], F32, kind="ExternalInput").ap()
    projw = nc.dram_tensor("projw", [WD, D], F32, kind="ExternalInput").ap()
    projb_c = nc.dram_tensor("projb_c", [128, 32], F32, kind="ExternalInput").ap()
    fwT = nc.dram_tensor("fwT", [D, D], F32, kind="ExternalInput").ap()
    fb_c = nc.dram_tensor("fb_c", [128, 32], F32, kind="ExternalInput").ap()
    gwT3 = nc.dram_tensor("gwT3", [128, 256], F32, kind="ExternalInput").ap()
    ics = nc.dram_tensor("ics", [8, D], F32, kind="ExternalInput").ap()
    ocs = nc.dram_tensor("ocs", [8, D], F32, kind="ExternalInput").ap()
    x_out = nc.dram_tensor("x_out", [T, D], F32, kind="ExternalOutput").ap()

    scr_fw = nc.dram_tensor("scr_fw", [64 * 128, 2048], BF16, kind="Internal").ap()

    identB = nc.inline_tensor(np.eye(128, dtype=BFNP), "identB").ap()
    identF = nc.inline_tensor(np.eye(128, dtype=np.float32), "identF").ap()
    ones_col = nc.inline_tensor(np.ones((128, 1), dtype=BFNP), "ones_col").ap()
    ones_row = nc.inline_tensor(np.ones((1, 128), dtype=BFNP), "ones_row").ap()
    mask_np = np.tril(np.ones((128, 128))).astype(BFNP)
    binmask = nc.inline_tensor(mask_np, "binmask").ap()
    epsc = nc.inline_tensor(np.full((1, 1), EPS, dtype=np.float32), "epsc").ap()

    with contextlib.ExitStack() as st:
        XR = st.enter_context(nc.sbuf_tensor("XR", [128, 32 * 1040], BF16))
        XN = st.enter_context(nc.sbuf_tensor("XN", [128, 32 * TH], BF16))
        ARENA = st.enter_context(nc.sbuf_tensor("ARENA", [128, 32768], BF16))
        ROUT = st.enter_context(nc.sbuf_tensor("ROUT", [8, TH], BF16))
        sb_convb = st.enter_context(nc.sbuf_tensor("sb_convb", [128, 24], F32))
        sb_bq = st.enter_context(nc.sbuf_tensor("sb_bq", [128, 8], F32))
        sb_bk = st.enter_context(nc.sbuf_tensor("sb_bk", [128, 8], F32))
        sb_bv = st.enter_context(nc.sbuf_tensor("sb_bv", [128, 8], F32))
        sb_bo = st.enter_context(nc.sbuf_tensor("sb_bo", [128, 8], F32))
        sb_pb = st.enter_context(nc.sbuf_tensor("sb_pb", [128, 32], F32))
        sb_fbh = st.enter_context(nc.sbuf_tensor("sb_fbh", [128, 32], F32))
        sb_gwT = st.enter_context(nc.sbuf_tensor("sb_gwT", [128, 256], BF16))
        sb_id = st.enter_context(nc.sbuf_tensor("sb_id", [128, 128], BF16))
        sb_idf = st.enter_context(nc.sbuf_tensor("sb_idf", [128, 128], F32))
        sb_ones = st.enter_context(nc.sbuf_tensor("sb_ones", [128, 1], BF16))
        sb_onesr = st.enter_context(nc.sbuf_tensor("sb_onesr", [1, 128], BF16))
        sb_mask = st.enter_context(nc.sbuf_tensor("sb_mask", [128, 128], BF16))
        sb_hm = st.enter_context(nc.sbuf_tensor("sb_hm", [1, TH], F32))
        sb_eps = st.enter_context(nc.sbuf_tensor("sb_eps", [1, 1], F32))

        PS = [st.enter_context(nc.psum_tensor(f"ps{i}", [128, 512], F32)) for i in range(8)]

        sW = st.enter_context(nc.semaphore("sW"))
        sHW = st.enter_context(nc.semaphore("sHW"))
        sPE = st.enter_context(nc.semaphore("sPE"))
        sDV = st.enter_context(nc.semaphore("sDV"))
        sAC = st.enter_context(nc.semaphore("sAC"))
        sPL = st.enter_context(nc.semaphore("sPL"))
        sOUT = st.enter_context(nc.semaphore("sOUT"))
        sGW = st.enter_context(nc.semaphore("sGW"))
        sTM = [st.enter_context(nc.semaphore(f"sTM{i}")) for i in range(2)]
        sWB = [st.enter_context(nc.semaphore(f"sWB{i}")) for i in range(6)]
        sCW = [st.enter_context(nc.semaphore(f"sCW{i}")) for i in range(4)]
        sOS = [st.enter_context(nc.semaphore(f"sOS{i}")) for i in range(2)]

        S = Sched()

        def snap():
            return [(sPE, S.val(sPE)), (sDV, S.val(sDV)), (sAC, S.val(sAC)),
                    (sPL, S.val(sPL))]

        ARB = ARENA.ap()
        ARF = ARB.bitcast(F32)

        def arb(off, n):
            return ARB[:, off // 2: off // 2 + n]

        def arf(off, n):
            return ARF[:, off // 4: off // 4 + n]

        def arbp(off, p, n):
            return ARB[0:p, off // 2: off // 2 + n]

        def arfp(off, p, n):
            return ARF[0:p, off // 4: off // 4 + n]

        XRa = XR.ap()
        XNa = XN.ap()
        ROUTa = ROUT.ap()

        def XRs(c, g0, n):
            return XRa[:, c * 1040 + g0: c * 1040 + g0 + n]

        def XNs(c, g0, n):
            return XNa[:, c * TH + g0: c * TH + g0 + n]

        def QKVs(c, g0, n):
            return ARB[:, c * 1024 + g0: c * 1024 + g0 + n]

        pb_bf = PS[7].ap().bitcast(BF16)
        pb_bf6 = PS[6].ap().bitcast(BF16)
        icsb = ARB[0:8, OFF_ICS // 2: OFF_ICS // 2 + D]

        ln_sq_free = [[], [], [], []]
        ln_tmp_free = [[], [], [], []]
        ln_bt_free = [[], [], []]
        ln_ps0_free = [[]]
        ln_ps1_free = [[]]

        for (dst, src) in [(sb_convb.ap(), convb_c), (sb_bq.ap(), bq_c),
                           (sb_bk.ap(), bk_c), (sb_bv.ap(), bv_c), (sb_bo.ap(), bo_c),
                           (sb_pb.ap(), projb_c), (sb_fbh.ap(), fb_c),
                           (sb_id.ap(), identB), (sb_idf.ap(), identF),
                           (sb_ones.ap(), ones_col), (sb_onesr.ap(), ones_row),
                           (sb_mask.ap(), binmask), (sb_hm.ap(), hmrow),
                           (sb_eps.ap(), epsc)]:
            S.op("sync", lambda e, d=dst, s=src: e.dma_start(out=d, in_=s),
                 incs=[(sHW, 16)])
        m_small = S.val(sHW)
        m_gw = S.op("gpsimd", lambda e: e.dma_start(out=sb_gwT.ap(), in_=gwT3),
                    incs=[(sGW, 16)])[0]
        m_fbh = S.op("scalar", lambda e: e.mul(sb_fbh.ap(), sb_fbh.ap(), 0.5),
                     waits=[(sHW, m_small)], incs=[(sAC, 1)])[0]
        m_bqs = S.op("scalar", lambda e: e.mul(sb_bq.ap(), sb_bq.ap(), QSCALE),
                     incs=[(sAC, 1)])[0]
        m_const = [(sHW, m_small), (sAC, m_bqs), (sGW, m_gw)]

        tm_free = [[], []]
        xr_ready = [None] * DC
        TTILES = [(t * 128, 128) for t in range(8)] + [(1024, 2)]
        pb_free_in = [[], []]
        for ti, (r0, nr) in enumerate(TTILES):
            tmv = ARB[0:128, (ti % 2) * 4096: (ti % 2) * 4096 + 4096]
            dv_ = S.op("gpsimd", lambda e, r0=r0, nr=nr, tmv=tmv:
                       e.dma_start(out=tmv[0:nr, :], in_=x_in[r0:r0 + nr, :]),
                       waits=list(tm_free[ti % 2]), incs=[(sTM[ti % 2], 16)])[0]
            lastT = None
            for fcg in range(4):
                pbv = pb_bf if fcg % 2 == 0 else pb_bf6
                for j in range(8):
                    fc = fcg * 8 + j
                    w = []
                    if j == 0:
                        w += list(pb_free_in[fcg % 2])
                        if fcg == 0:
                            w.append((sTM[ti % 2], dv_))
                        if ti == 0 and fcg == 0:
                            w.append((sHW, m_small))

                    def ft(e, tmv=tmv, nr=nr, fc=fc, j=j, pbv=pbv):
                        return e.transpose(pbv[:, j * 128: j * 128 + nr],
                                           tmv[0:nr, fc * 128:(fc + 1) * 128],
                                           sb_id.ap()[0:nr, 0:nr])
                    r = S.op("tensor", ft, waits=w,
                             incs=[(sPE, 1)] if j == 7 else [])
                    if j == 7:
                        lastT = r[0]
                for j in range(8):
                    fc = fcg * 8 + j

                    def fe(e, fc=fc, j=j, r0=r0, nr=nr, pbv=pbv):
                        return e.activation(
                            XRa[:, fc * 1040 + r0: fc * 1040 + r0 + nr],
                            pbv[:, j * 128: j * 128 + nr], AF.Copy)
                    ev_ = S.op("scalar", fe, waits=[(sPE, lastT)] if j == 0 else [],
                               incs=[(sAC, 1)])[0]
                    if ti == len(TTILES) - 1:
                        xr_ready[fc] = (sAC, ev_)
                pb_free_in[fcg % 2] = [(sAC, S.val(sAC))]
            tm_free[ti % 2] = [(sPE, lastT)]


        ln_group_done_tmp = []

        def _emit_applies(ev, gi):
            (g0, n, rbt, mrbt, e1, e2) = ev
            for c in range(DC):
                tA = arb(OFF_TMPA[c % 4], 512)[:, 0:n]
                w = [(sAC, e1)] + list(ln_tmp_free[c % 4])
                vv = S.op("vector", lambda e, tA=tA, c=c, g0=g0, n=n, rbt=rbt:
                          e.tensor_mul(tA, XRs(c, g0, n), rbt),
                          waits=w, incs=[(sDV, 1)])[0]
                pv = S.op("gpsimd", lambda e, tA=tA, c=c, g0=g0, n=n, mrbt=mrbt:
                          e.tensor_sub(XNs(c, g0, n), tA, mrbt),
                          waits=[(sDV, vv), (sAC, e2)], incs=[(sPL, 1)])[0]
                ln_tmp_free[c % 4] = [(sPL, pv)]
            ln_bt_free[gi] = [(sPL, S.val(sPL))]
            ln_group_done_tmp.append((sPL, S.val(sPL)))

        def emit_ln(hmask, entry_waits, xr_chunk_ready=None):
            ln_group_done_tmp.clear()
            evs = []
            for gi, (g0, n) in enumerate(W3G):
                for c in range(DC):
                    w = []
                    if gi == 0 and c == 0:
                        w += list(entry_waits)
                    if xr_chunk_ready is not None and gi == 0:
                        w.append(xr_chunk_ready[c])
                    if c == 0:
                        w += list(ln_ps0_free[0])

                    def f(e, c=c, g0=g0, n=n, first=(c == 0), last=(c == DC - 1)):
                        return e.matmul(PS[0].ap()[0:1, 0:n], sb_ones.ap(),
                                        XRs(c, g0, n), start=first, stop=last)
                    S.op("tensor", f, waits=w,
                         incs=[(sPE, 1)] if c == DC - 1 else [])
                m_sum = S.val(sPE)
                for c in range(DC):
                    sq = arb(OFF_SQ[c % 4], 512)[:, 0:n]
                    w = list(ln_sq_free[c % 4])
                    if gi == 0 and c <= 1:
                        w += list(entry_waits)
                    if xr_chunk_ready is not None and gi == 0:
                        w.append(xr_chunk_ready[c])
                    sq_eng = "vector" if c % 2 == 0 else "gpsimd"
                    sq_sem = sDV if c % 2 == 0 else sPL

                    def fs(e, c=c, g0=g0, n=n, sq=sq):
                        return e.tensor_mul(sq, XRs(c, g0, n), XRs(c, g0, n))
                    vs = S.op(sq_eng, fs, waits=w, incs=[(sq_sem, 1)])[0]

                    wm = [(sq_sem, vs)]
                    if c == 0:
                        wm += list(ln_ps1_free[0])

                    def fm(e, sq=sq, n=n, first=(c == 0), last=(c == DC - 1)):
                        return e.matmul(PS[1].ap()[0:1, 0:n], sb_ones.ap(), sq,
                                        start=first, stop=last)
                    vm = S.op("tensor", fm, waits=wm, incs=[(sPE, 1)])[0]
                    ln_sq_free[c % 4] = [(sPE, vm)]
                m_ssq = S.val(sPE)

                m_row = arfp(OFF_MROW, 1, n)
                v_row = arfp(OFF_VROW, 1, n)
                r_row = arfp(OFF_RROW, 1, n)
                mr_row = arfp(OFF_MRROW, 1, n)
                rbf = arbp(OFF_RBFROW, 1, n)
                mrbf = arbp(OFF_MRBFROW, 1, n)
                v1 = S.op("vector", lambda e, m_row=m_row, n=n:
                          e.tensor_scalar_mul(m_row, PS[0].ap()[0:1, 0:n], INV_D),
                          waits=[(sPE, m_sum)], incs=[(sDV, 1)])[0]
                ln_ps0_free[0] = [(sDV, v1)]
                v2 = S.op("vector", lambda e, mr_row=mr_row, m_row=m_row:
                          e.tensor_mul(mr_row, m_row, m_row),
                          waits=[(sDV, v1)], incs=[(sDV, 1)])[0]
                v3 = S.op("vector", lambda e, v_row=v_row, mr_row=mr_row, n=n:
                          e.scalar_tensor_tensor(v_row, PS[1].ap()[0:1, 0:n], INV_D,
                                                 mr_row, ALU.mult, ALU.subtract),
                          waits=[(sDV, v2), (sPE, m_ssq)], incs=[(sDV, 1)])[0]
                ln_ps1_free[0] = [(sDV, v3)]
                a1 = S.op("scalar", lambda e, mr_row=mr_row, v_row=v_row:
                          e.activation(mr_row, v_row, AF.Sqrt, bias=sb_eps.ap()),
                          waits=[(sDV, v3)], incs=[(sAC, 1)])[0]
                v4 = S.op("vector", lambda e, r_row=r_row, mr_row=mr_row:
                          e.reciprocal(r_row, mr_row),
                          waits=[(sAC, a1)], incs=[(sDV, 1)])[0]
                if hmask:
                    v4 = S.op("vector", lambda e, r_row=r_row, g0=g0, n=n:
                              e.tensor_mul(r_row, r_row, sb_hm.ap()[0:1, g0:g0 + n]),
                              waits=[(sDV, v4)], incs=[(sDV, 1)])[0]
                v5 = S.op("vector", lambda e, mr_row=mr_row, m_row=m_row, r_row=r_row:
                          e.tensor_mul(mr_row, m_row, r_row),
                          waits=[(sDV, v4)], incs=[(sDV, 1)])[0]
                v6 = S.op("vector", lambda e, rbf=rbf, r_row=r_row:
                          e.tensor_copy(rbf, r_row),
                          waits=[(sDV, v5)], incs=[(sDV, 1)])[0]
                v7 = S.op("vector", lambda e, mrbf=mrbf, mr_row=mr_row:
                          e.tensor_copy(mrbf, mr_row),
                          waits=[(sDV, v6)], incs=[(sDV, 1)])[0]
                p1 = S.op("tensor", lambda e, n=n, rbf=rbf:
                          e.matmul(PS[2].ap()[:, 0:n], sb_onesr.ap(), rbf,
                                   start=True, stop=True),
                          waits=[(sDV, v7), (sAC, S.val(sAC))], incs=[(sPE, 1)])[0]
                p2 = S.op("tensor", lambda e, n=n, mrbf=mrbf:
                          e.matmul(PS[3].ap()[:, 0:n], sb_onesr.ap(), mrbf,
                                   start=True, stop=True),
                          incs=[(sPE, 1)])[0]
                rbt = arb(OFF_RBT[gi], 512)[:, 0:n]
                mrbt = arb(OFF_MRBT[gi], 512)[:, 0:n]
                e1 = S.op("scalar", lambda e, rbt=rbt, n=n:
                          e.activation(rbt, PS[2].ap()[:, 0:n], AF.Copy),
                          waits=[(sPE, p1)] + list(ln_bt_free[gi]), incs=[(sAC, 1)])[0]
                e2 = S.op("scalar", lambda e, mrbt=mrbt, n=n:
                          e.activation(mrbt, PS[3].ap()[:, 0:n], AF.Copy),
                          waits=[(sPE, p2)], incs=[(sAC, 1)])[0]
                evs.append((g0, n, rbt, mrbt, e1, e2))
                if gi >= 1:
                    _emit_applies(evs[gi - 1], gi - 1)
            _emit_applies(evs[len(W3G) - 1], len(W3G) - 1)
            return (sPL, S.val(sPL)), list(ln_group_done_tmp)

        _UNUSED = """
            group_done = []
            for gi, (g0, n, rbt, mrbt, e1, e2) in enumerate(evs):
                for c in range(DC):
                    tA = arb(OFF_TMPA[c % 4], 512)[:, 0:n]
                    w = [(sAC, e1)] + list(ln_tmp_free[c % 4])
                    vv = S.op("vector", lambda e, tA=tA, c=c, g0=g0, n=n, rbt=rbt:
                              e.tensor_mul(tA, XRs(c, g0, n), rbt),
                              waits=w, incs=[(sDV, 1)])[0]
                    pv = S.op("gpsimd", lambda e, tA=tA, c=c, g0=g0, n=n, mrbt=mrbt:
                              e.tensor_sub(XNs(c, g0, n), tA, mrbt),
                              waits=[(sDV, vv), (sAC, e2)], incs=[(sPL, 1)])[0]
                    ln_tmp_free[c % 4] = [(sPL, pv)]
                ln_bt_free[gi] = [(sPL, S.val(sPL))]
                group_done.append((sPL, S.val(sPL)))
            return (sPL, S.val(sPL)), group_done
        """

        def emit_router(group_gates):
            for gi, (g0, n) in enumerate(W3G):
                for c in range(DC):
                    w = []
                    if c == 0:
                        w.append(group_gates[gi])
                        if gi == 0:
                            w.append((sGW, m_gw))
                    if c == 0:
                        w.append((sAC, S.val(sAC)))

                    def f(e, c=c, g0=g0, n=n, first=(c == 0), last=(c == DC - 1)):
                        return e.matmul(PS[6].ap()[0:8, 0:n],
                                        sb_gwT.ap()[:, c * 8:(c + 1) * 8],
                                        XNs(c, g0, n), start=first, stop=last)
                    S.op("tensor", f, waits=w, incs=[(sPE, 1)] if c == DC - 1 else [])
                m_lg = S.val(sPE)
                lg = arfp(OFF_LG, 8, 512)[:, 0:n]
                e_lg = S.op("scalar", lambda e, lg=lg, n=n:
                            e.activation(lg, PS[6].ap()[0:8, 0:n], AF.Copy),
                            waits=[(sPE, m_lg)], incs=[(sAC, 1)])[0]
                prev_exp = 0
                ntile = (n + 127) // 128
                for t in range(ntile):
                    tn = min(128, n - t * 128)
                    w = [(sAC, e_lg)]
                    if prev_exp:
                        w.append((sAC, prev_exp))

                    def t1(e, lg=lg, t=t, tn=tn):
                        return e.transpose(PS[5].ap()[0:tn, 0:8],
                                           lg[:, t * 128: t * 128 + tn],
                                           sb_idf.ap()[0:8, 0:8])
                    t1v = S.op("tensor", t1, waits=w, incs=[(sPE, 1)])[0]
                    nm = arfp(OFF_RCOLS, 128, 8)[0:tn, 0:1]
                    rs = arfp(OFF_RCOLS, 128, 8)[0:tn, 1:2]
                    rc2 = arfp(OFF_RCOLS, 128, 8)[0:tn, 2:3]
                    rv = S.op("vector", lambda e, nm=nm, tn=tn:
                              e.reduce_max(nm, PS[5].ap()[0:tn, 0:8], axis=AX.X,
                                           negate=True),
                              waits=[(sPE, t1v)], incs=[(sDV, 1)])[0]
                    pbuf = arbp(OFF_PBUF[t % 2], 128, 8)[0:tn, :]
                    av = S.op("scalar", lambda e, pbuf=pbuf, nm=nm, rs=rs, tn=tn:
                              e.activation(pbuf, PS[5].ap()[0:tn, 0:8], AF.Exp,
                                           bias=nm, accum_out=rs),
                              waits=[(sDV, rv), (sPE, S.val(sPE))], incs=[(sAC, 1)])[0]
                    prev_exp = av
                    rc = S.op("vector", lambda e, rs=rs, rc2=rc2:
                              e.reciprocal(rc2, rs),
                              waits=[(sAC, av)], incs=[(sDV, 1)])[0]
                    pm = S.op("scalar", lambda e, pbuf=pbuf, rc2=rc2:
                              e.activation(pbuf, pbuf, AF.Identity, scale=rc2),
                              waits=[(sDV, rc)], incs=[(sAC, 1)])[0]
                    t2v = S.op("tensor", lambda e, pbuf=pbuf, tn=tn:
                               e.transpose(pb_bf[0:8, 0:tn], pbuf,
                                           sb_id.ap()[0:tn, 0:tn]),
                               waits=[(sAC, pm)], incs=[(sPE, 1)])[0]
                    S.op("scalar", lambda e, g0=g0, t=t, tn=tn:
                         e.activation(ROUTa[0:8, g0 + t * 128: g0 + t * 128 + tn],
                                      pb_bf[0:8, 0:tn], AF.Copy),
                         waits=[(sPE, t2v)], incs=[(sAC, 1)])[0]
            return (sAC, S.val(sAC))

        def emit_xin(entry_waits):
            bank_free = {4: [], 5: []}
            for gi, (g0, n) in enumerate(W3G):
                for c in range(DC):
                    bk = 4 + (c % 2)
                    w = list(bank_free[bk])
                    if gi == 0 and c == 0:
                        w += list(entry_waits)

                    def f(e, c=c, g0=g0, n=n, bk=bk):
                        return e.matmul(PS[bk].ap()[:, 0:n],
                                        icsb[0:8, c * 128:(c + 1) * 128],
                                        ROUTa[0:8, g0:g0 + n], start=True, stop=True)
                    mv = S.op("tensor", f, waits=w, incs=[(sPE, 1)])[0]
                    dv = S.op("vector", lambda e, c=c, g0=g0, n=n, bk=bk:
                              e.tensor_mul(XNs(c, g0, n), XNs(c, g0, n),
                                           PS[bk].ap()[:, 0:n]),
                              waits=[(sPE, mv)], incs=[(sDV, 1)])[0]
                    bank_free[bk] = [(sDV, dv)]
            return (sDV, S.val(sDV))

        def emit_ffn_main(first, groups, entry_waits, prefetch_free=None,
                          ocs_load=None, ocs_gate=None):
            ng = len(groups)
            base = snap()
            init = list(prefetch_free) if prefetch_free is not None else []
            wbtb_free = [list(init) for _ in range(6)]
            bank_free = {i: list(base) for i in range(6)}
            osh_free = [[], []]
            tsb_free = [[], []]
            t2_free = [[], []]
            ps6_free = list(base)
            ew = list(entry_waits)
            wbt_ready = {}
            chunk_done = [None] * DC

            def emit_weights(o):
                rdy = []
                for hf in range(2):
                    slot = (o * 2 + hf) % 6
                    wbtb = arb(OFF_WBTB[slot], 2048)
                    if first:
                        def fd(e, o=o, hf=hf, wbtb=wbtb):
                            srcp = fwT[hf * 2048:(hf + 1) * 2048, o * 128:(o + 1) * 128]
                            return e.dma_start(
                                out=wbtb.rearrange("p (c o) -> p c o", c=16),
                                in_=srcp.rearrange("(c p) o -> p c o", p=128))
                        dv_ = S.op("gpsimd", fd, waits=list(wbtb_free[slot]),
                                   incs=[(sWB[slot], 16)])[0]
                        sv = S.op("vector", lambda e, wbtb=wbtb:
                                  e.tensor_scalar(wbtb.bitcast(I16), wbtb.bitcast(I16),
                                                  -32768, 0x3F80,
                                                  ALU.bitwise_and, ALU.bitwise_or),
                                  waits=[(sWB[slot], dv_)], incs=[(sDV, 1)])[0]
                        hv = S.op("sync", lambda e, o=o, hf=hf, wbtb=wbtb:
                                  e.dma_start(
                                      out=scr_fw[(o * 2 + hf) * 128:(o * 2 + hf + 1) * 128, :],
                                      in_=wbtb),
                                  waits=[(sDV, sv)], incs=[(sWB[slot], 16)])[0]
                        rdy.append((sDV, sv))
                        wbtb_free[slot] = [(sWB[slot], hv)]
                    else:
                        hv = S.op("sync", lambda e, o=o, hf=hf, wbtb=wbtb:
                                  e.dma_start(
                                      out=wbtb,
                                      in_=scr_fw[(o * 2 + hf) * 128:(o * 2 + hf + 1) * 128, :]),
                                  waits=list(wbtb_free[slot]), incs=[(sWB[slot], 16)])[0]
                        rdy.append((sWB[slot], hv))
                        wbtb_free[slot] = []
                wbt_ready[o] = rdy

            if PREFETCH_W:
                emit_weights(0)
                if PF_DIST >= 2:
                    emit_weights(1)
            if ocs_load is not None:
                ocs_load()
            for o in range(DC):
                if not PREFETCH_W:
                    emit_weights(o)
                last_mm = {}
                for hf in range(2):
                    slot = (o * 2 + hf) % 6
                    wbtb = arb(OFF_WBTB[slot], 2048)
                    for gi, (g0, n) in enumerate(groups):
                        bk = (o % 2) * 3 + gi
                        for kk in range(16):
                            k = hf * 16 + kk
                            w = []
                            if kk == 0:
                                w.append(wbt_ready[o][hf])
                                if hf == 0:
                                    w += list(bank_free[bk])
                                    if o == 0 and gi == 0:
                                        w += ew

                            def fm(e, kk=kk, k=k, g0=g0, n=n, bk=bk, wbtb=wbtb):
                                return e.matmul(PS[bk].ap()[:, 0:n],
                                                wbtb[:, kk * 128:(kk + 1) * 128],
                                                XNs(k, g0, n),
                                                start=(k == 0), stop=(k == 31))
                            incs = [(sPE, 1)] if (hf == 1 and kk == 15) else []
                            r = S.op("tensor", fm, waits=w, incs=incs)
                            if incs:
                                last_mm[gi] = r[0]
                for hf in range(2):
                    wbtb_free[(o * 2 + hf) % 6].append((sPE, last_mm[ng - 1]))
                if PREFETCH_W and PF_EARLY and o + PF_DIST < DC:
                    emit_weights(o + PF_DIST)
                for gi, (g0, n) in enumerate(groups):
                    bk = (o % 2) * 3 + gi

                    def fo(e, o=o, g0=g0, n=n):
                        return e.matmul(PS[6].ap()[:, 0:n],
                                        icsb[0:8, o * 128:(o + 1) * 128],
                                        ROUTa[0:8, g0:g0 + n], start=True, stop=True)
                    wfo = list(ps6_free)
                    if o == 0 and gi == 0 and ocs_gate is not None:
                        wfo += list(ocs_gate)
                    ov = S.op("tensor", fo, waits=wfo, incs=[(sPE, 1)])[0]
                    osh = arf(OFF_OSH[gi % 2], 512)[:, 0:n]
                    av = S.op("scalar", lambda e, osh=osh, n=n:
                              e.activation(osh, PS[6].ap()[:, 0:n], AF.Copy, scale=0.5),
                              waits=[(sPE, ov)] + list(osh_free[gi % 2]),
                              incs=[(sAC, 1)])[0]
                    ps6_free = [(sAC, av)]
                    tsb = arf(OFF_TSB[gi % 2], 512)[:, 0:n]
                    tv = S.op("vector", lambda e, tsb=tsb, osh=osh, n=n, bk=bk:
                              e.tensor_mul(tsb, PS[bk].ap()[:, 0:n], osh),
                              waits=[(sPE, last_mm[gi]), (sAC, av)] + list(tsb_free[gi % 2]),
                              incs=[(sDV, 1)])[0]
                    bank_free[bk] = [(sDV, tv)]
                    osh_free[gi % 2] = [(sDV, tv)]
                    t2 = arf(OFF_T2[gi % 2], 512)[:, 0:n]
                    a2 = S.op("scalar", lambda e, t2=t2, tsb=tsb, o=o:
                              e.activation(t2, tsb, AF.Identity, bias=sb_fbh.ap()[:, o:o + 1]),
                              waits=[(sDV, tv)] + list(t2_free[gi % 2]),
                              incs=[(sAC, 1)])[0]
                    tsb_free[gi % 2] = [(sAC, a2)]
                    pv = S.op("vector", lambda e, t2=t2, o=o, g0=g0, n=n:
                              e.tensor_add(XRs(o, g0, n), XRs(o, g0, n), t2),
                              waits=[(sAC, a2)], incs=[(sDV, 1)])[0]
                    t2_free[gi % 2] = [(sDV, pv)]
                    chunk_done[o] = (sDV, pv)
                if PREFETCH_W and not PF_EARLY and o + PF_DIST < DC:
                    emit_weights(o + PF_DIST)
            return (sDV, S.val(sDV)), chunk_done

        m_ics = S.op("gpsimd", lambda e: e.dma_start(out=icsb, in_=ics),
                     incs=[(sW, 16)])[0]

        tap_src = [None]
        tap_m = [None]
        if tap == 'load':
            tap_src[0] = 'XR'
            tap_m[0] = xr_ready[DC - 1]
        if tap_m[0] is None:
            m_ln1, lng1 = emit_ln(hmask=False, entry_waits=m_const,
                                  xr_chunk_ready=xr_ready)
            if tap == 'ln1':
                tap_src[0] = 'XN'
                tap_m[0] = m_ln1
        if tap_m[0] is None:
            m_rt1 = emit_router(lng1)
            m_xin1 = emit_xin(entry_waits=[m_rt1, (sW, m_ics)])
            if tap == 'xin1':
                tap_src[0] = 'XN'
                tap_m[0] = m_xin1
        if tap_m[0] is None:
            xin_pe = (sPE, S.val(sPE))
            ocs_ready = []

            def load_ocs():
                v = S.op("gpsimd", lambda e: e.dma_start(out=icsb, in_=ocs),
                         waits=[xin_pe], incs=[(sW, 16)])[0]
                ocs_ready.append((sW, v))
            m_ffn1, ffn1_cd = emit_ffn_main(first=True, groups=W3G,
                                            entry_waits=[m_xin1],
                                            prefetch_free=[tm_free[0][0], tm_free[1][0]],
                                            ocs_load=load_ocs,
                                            ocs_gate=ocs_ready)
        else:
            upto = 0
            m_ffn1 = tap_m[0]

        if upto >= 2:
            m_ln2, _lng2 = emit_ln(hmask=True, entry_waits=[],
                                   xr_chunk_ready=ffn1_cd)
            if tap == 'ln2':
                tap_src[0] = 'XN'
                tap_m[0] = m_ln2

            skip2 = tap_m[0] is not None
            cw_free = [[m_rt1], [m_rt1], [m_rt1], [m_rt1]]
            base = snap()
            bank_free_c = {i: list(base) for i in range(8)}
            uslot = 0
            for s3 in range(3 if not skip2 else 0):
                for qt in range(4):
                    mmcount = {}
                    lastv = {}
                    unit_last = {}
                    for u in range(24):
                        cvt = u // 8
                        k4 = u % 8
                        slot = uslot % 4
                        uslot += 1
                        cw = arb(OFF_CWR[slot], 1024)

                        def fd(e, cvt=cvt, k4=k4, s3=s3, qt=qt, cw=cw):
                            src = conv_w[cvt, k4 * 512:(k4 + 1) * 512,
                                         s3 * 1024 + qt * 256: s3 * 1024 + qt * 256 + 256]
                            return e.dma_start(
                                out=cw.rearrange("p (c o) -> p c o", c=4),
                                in_=src.rearrange("(c p) o -> p c o", p=128))
                        dv_ = S.op("gpsimd", fd, waits=list(cw_free[slot]),
                                   incs=[(sCW[slot], 16)])[0]
                        nmm = 0
                        for kk in range(4):
                            k = k4 * 4 + kk
                            for oc2 in range(2):
                                for gi, (g0, n) in enumerate(CG):
                                    bk = (qt % 2) * 4 + oc2 * 2 + gi
                                    key = (oc2, gi)
                                    cnt = mmcount.get(key, 0)
                                    mmcount[key] = cnt + 1
                                    nmm += 1
                                    w = []
                                    if kk == 0 and oc2 == 0 and gi == 0:
                                        w.append((sCW[slot], dv_))
                                    if cnt == 0:
                                        w += list(bank_free_c[bk])
                                        if s3 == 0 and qt == 0:
                                            w.append(m_ln2)
                                    last_in_unit = (nmm == 16)

                                    def fm(e, cw=cw, kk=kk, oc2=oc2, g0=g0, n=n,
                                           cvt=cvt, k=k, bk=bk,
                                           first=(cnt == 0), last=(cnt == 95)):
                                        return e.matmul(
                                            PS[bk].ap()[:, 0:n],
                                            cw[:, kk * 256 + oc2 * 128:
                                               kk * 256 + oc2 * 128 + 128],
                                            XNs(k, g0 + cvt, n),
                                            start=first, stop=last)
                                    incs = []
                                    if cnt == 95:
                                        incs.append((sPE, 1))
                                    elif last_in_unit:
                                        incs.append((sPE, 1))
                                    r = S.op("tensor", fm, waits=w, incs=incs)
                                    if cnt == 95:
                                        lastv[key] = r[0]
                                    if last_in_unit:
                                        unit_last[slot] = r[0]
                        cw_free[slot] = [(sPE, unit_last[slot])]
                    for oc2 in range(2):
                        for gi, (g0, n) in enumerate(CG):
                            bk = (qt % 2) * 4 + oc2 * 2 + gi
                            ocg = s3 * 8 + qt * 2 + oc2

                            def fe(e, bk=bk, ocg=ocg, g0=g0, n=n):
                                return e.activation(QKVs(ocg, g0, n),
                                                    PS[bk].ap()[:, 0:n], AF.Identity,
                                                    bias=sb_convb.ap()[:, ocg:ocg + 1])
                            ev_ = S.op("scalar", fe, waits=[(sPE, lastv[(oc2, gi)])],
                                       incs=[(sAC, 1)])[0]
                            bank_free_c[bk] = [(sAC, ev_)]
            m_conv = (sAC, S.val(sAC))

            pw_free = [list(cw_free[i]) for i in range(4)]
            base = snap()
            bank_free_p = {i: list(base) for i in range(4)}
            pslot = 0
            hp_list = [] if skip2 else [(wq, sb_bq), (wk, sb_bk), (wv, sb_bv)]
            for pi, (wsrc_d, bias_sb) in enumerate(hp_list):
                for j in range(8):
                    slot = pslot % 4
                    pslot += 1
                    cw = arb(OFF_CWR[slot], 1024)

                    def fd(e, wsrc_d=wsrc_d, j=j, cw=cw):
                        return e.dma_start(
                            out=cw.rearrange("p (c o) -> p c o", c=8),
                            in_=wsrc_d[:, j * 128:(j + 1) * 128]
                            .rearrange("(c p) o -> p c o", p=128))
                    dv_ = S.op("gpsimd", fd, waits=list(pw_free[slot]),
                               incs=[(sCW[slot], 16)])[0]
                    lastg = {}
                    for gi, (g0, n) in enumerate(CG):
                        bk = (j % 2) * 2 + gi
                        for k in range(8):
                            w = []
                            if k == 0:
                                w.append((sCW[slot], dv_))
                                w += list(bank_free_p[bk])
                                if pi == 0 and j == 0 and gi == 0:
                                    w.append(m_conv)

                            def fm(e, cw=cw, k=k, g0=g0, n=n, bk=bk, pi=pi):
                                return e.matmul(PS[bk].ap()[:, 0:n],
                                                cw[:, k * 128:(k + 1) * 128],
                                                QKVs(pi * 8 + k, g0, n),
                                                start=(k == 0), stop=(k == 7))
                            incs = [(sPE, 1)] if k == 7 else []
                            r = S.op("tensor", fm, waits=w, incs=incs)
                            if incs:
                                lastg[gi] = r[0]
                    pw_free[slot] = [(sPE, lastg[1])]
                    for gi, (g0, n) in enumerate(CG):
                        bk = (j % 2) * 2 + gi

                        def fe(e, bk=bk, dstc=pi * 8 + j, g0=g0, n=n,
                               bias_sb=bias_sb, j=j, pi=pi):
                            return e.activation(XNs(dstc, g0, n), PS[bk].ap()[:, 0:n],
                                                AF.Identity, bias=bias_sb.ap()[:, j:j + 1],
                                                scale=(QSCALE if pi == 0 else 1.0))
                        ev_ = S.op("scalar", fe, waits=[(sPE, lastg[gi])],
                                   incs=[(sAC, 1)])[0]
                        bank_free_p[bk] = [(sAC, ev_)]
            m_hproj = (sAC, S.val(sAC))
            if tap == 'hproj':
                tap_src[0] = 'XN0'
                tap_m[0] = m_hproj
            skip2 = tap_m[0] is not None

            base = snap()
            sc_bank_free = {i: list(base) for i in range(4)}
            o_bank_free = {4: list(base), 5: list(base)}
            pb_half_free = [list(base), list(base)]
            pn_free = [[], [], [], []]
            acol = arfp(OFF_ACOL, 128, 64)
            vhT = arb(OFF_VHT, 1024)
            pts = arb(OFF_PTS, 2048)
            for b in range(NB if not skip2 else 0):
                cb = b * BL
                for c8 in range(8):
                    w = list(pb_half_free[0]) + list(pb_half_free[1]) if c8 == 0 else []
                    if b == 0 and c8 == 0:
                        w.append(m_hproj)

                    def ft(e, c8=c8, cb=cb):
                        return e.transpose(
                            pb_bf[:, c8 * 128:(c8 + 1) * 128],
                            XNa[:, (16 + c8) * TH + cb:(16 + c8) * TH + cb + BL],
                            sb_id.ap())
                    r = S.op("tensor", ft, waits=w, incs=[(sPE, 1)] if c8 == 7 else [])
                    if c8 == 7:
                        vT_done = r[0]
                vh_ev = S.op("scalar", lambda e, vhT=vhT:
                             e.activation(vhT, pb_bf[:, :], AF.Copy),
                             waits=[(sPE, vT_done)], incs=[(sAC, 1)])[0]
                pb_half_free[0] = [(sAC, vh_ev)]
                pb_half_free[1] = [(sAC, vh_ev)]
                pts_ev = []
                for bkq in range(4):
                    lastmm = None
                    for idx in range(4):
                        h = bkq + idx * 4
                        w = list(sc_bank_free[bkq]) if idx == 0 else []

                        def fm(e, h=h, idx=idx, cb=cb, bkq=bkq):
                            pbase = 64 * (h % 2)
                            qh = XNa[pbase:pbase + 64,
                                     (h // 2) * TH + cb:(h // 2) * TH + cb + BL]
                            kh = XNa[pbase:pbase + 64,
                                     (8 + h // 2) * TH + cb:(8 + h // 2) * TH + cb + BL]
                            return e.matmul(PS[bkq].ap()[:, idx * 128:(idx + 1) * 128],
                                            qh, kh, start=True, stop=True)
                        r = S.op("tensor", fm, waits=w,
                                 incs=[(sPE, 1)] if idx == 3 else [])
                        if idx == 3:
                            lastmm = r[0]
                    exp_last = 0
                    pn_ready = {}
                    for idx in range(4):
                        h = bkq + idx * 4
                        cslot = idx + 4 * (bkq % 2)
                        nm = acol[:, cslot: cslot + 1]
                        rs = acol[:, 8 + cslot: 9 + cslot]
                        rc2 = acol[:, 16 + cslot: 17 + cslot]
                        rv = S.op("vector", lambda e, nm=nm, bkq=bkq, idx=idx:
                                  e.reduce_max(nm,
                                               PS[bkq].ap()[:, idx * 128:(idx + 1) * 128],
                                               axis=AX.X, negate=True),
                                  waits=[(sPE, lastmm)], incs=[(sDV, 1)])[0]
                        pn = arb(OFF_PN[idx], 128)
                        av = S.op("scalar", lambda e, pn=pn, nm=nm, bkq=bkq, idx=idx:
                                  e.activation(pn,
                                               PS[bkq].ap()[:, idx * 128:(idx + 1) * 128],
                                               AF.Exp, bias=nm),
                                  waits=[(sDV, rv)] + list(pn_free[idx]),
                                  incs=[(sAC, 1)])[0]
                        exp_last = av
                        pv = S.op("vector", lambda e, pn=pn, rs=rs:
                                  e.scalar_tensor_tensor(pn, pn, 1.0, sb_mask.ap(),
                                                         ALU.mult, ALU.mult,
                                                         accum_out=rs),
                                  waits=[(sAC, av)], incs=[(sDV, 1)])[0]
                        rc = S.op("vector", lambda e, rs=rs, rc2=rc2:
                                  e.reciprocal(rc2, rs),
                                  waits=[(sDV, pv)], incs=[(sDV, 1)])[0]
                        pnv = S.op("scalar", lambda e, pn=pn, rc2=rc2:
                                   e.activation(pn, pn, AF.Identity, scale=rc2),
                                   waits=[(sDV, rc)], incs=[(sAC, 1)])[0]
                        pn_ready[idx] = pnv
                    sc_bank_free[bkq] = [(sAC, exp_last)]
                    half = bkq % 2
                    pbv = pb_bf if half == 0 else pb_bf6
                    ptlast = None
                    for idx in range(4):
                        h = bkq + idx * 4
                        pn = arb(OFF_PN[idx], 128)
                        w = [(sAC, pn_ready[idx])]
                        if idx == 0:
                            w += list(pb_half_free[half])

                        def ft(e, pn=pn, idx=idx, pbv=pbv):
                            return e.transpose(
                                pbv[:, idx * 128:(idx + 1) * 128],
                                pn, sb_id.ap())
                        r = S.op("tensor", ft, waits=w, incs=[(sPE, 1)])
                        pn_free[idx] = [(sPE, r[0])]
                        ptlast = r[0]
                    ev_ = S.op("scalar", lambda e, bkq=bkq, pbv=pbv:
                               e.activation(pts[:, bkq * 512:(bkq + 1) * 512],
                                            pbv[:, 0:512],
                                            AF.Copy),
                               waits=[(sPE, ptlast)], incs=[(sAC, 1)])[0]
                    pb_half_free[half] = [(sAC, ev_)]
                    pts_ev.append(ev_)
                o_last = {4: None, 5: None}
                for hh in range(H):
                    c = hh // 2
                    bk = 4 + (c % 2)
                    w = []
                    if hh == 0:
                        w += list(o_bank_free[4]) + list(o_bank_free[5])
                        w.append((sAC, pts_ev[3]))
                        w.append((sAC, vh_ev))

                    def fm(e, hh=hh, c=c, bk=bk):
                        pcol = (hh % 4) * 512 + (hh // 4) * 128
                        return e.matmul(
                            PS[bk].ap()[64 * (hh % 2):64 * (hh % 2) + 64,
                                        (c // 2) * 128:(c // 2) * 128 + 128],
                            vhT[:, hh * 64:(hh + 1) * 64],
                            pts[:, pcol: pcol + 128],
                            start=True, stop=True)
                    is_last = hh in (13, 15)
                    r = S.op("tensor", fm, waits=w,
                             incs=[(sPE, 1)] if is_last else [])
                    if is_last:
                        o_last[bk] = r[0]
                for c in range(8):
                    bk = 4 + (c % 2)

                    def fe(e, c=c, bk=bk, cb=cb):
                        return e.activation(
                            XNa[:, c * TH + cb: c * TH + cb + BL],
                            PS[bk].ap()[:, (c // 2) * 128:(c // 2) * 128 + 128],
                            AF.Copy)
                    ev_ = S.op("scalar", fe, waits=[(sPE, o_last[bk])],
                               incs=[(sAC, 1)])[0]
                    o_bank_free[bk] = [(sAC, ev_)]
            m_attn = (sAC, S.val(sAC))
            if tap == 'attn':
                tap_src[0] = 'XN0'
                tap_m[0] = m_attn
            skip2 = tap_m[0] is not None

            base = snap()
            bank_free_w = {i: list(base) for i in range(4)}
            for j in range(8 if not skip2 else 0):
                slot = pslot % 4
                pslot += 1
                cw = arb(OFF_CWR[slot], 1024)

                def fd(e, j=j, cw=cw):
                    return e.dma_start(
                        out=cw.rearrange("p (c o) -> p c o", c=8),
                        in_=wo[:, j * 128:(j + 1) * 128]
                        .rearrange("(c p) o -> p c o", p=128))
                dv_ = S.op("gpsimd", fd, waits=list(pw_free[slot]),
                           incs=[(sCW[slot], 16)])[0]
                lastg = {}
                for gi, (g0, n) in enumerate(CG):
                    bk = (j % 2) * 2 + gi
                    for k in range(8):
                        w = []
                        if k == 0:
                            w.append((sCW[slot], dv_))
                            w += list(bank_free_w[bk])
                            if j == 0 and gi == 0:
                                w.append(m_attn)

                        def fm(e, cw=cw, k=k, g0=g0, n=n, bk=bk):
                            return e.matmul(PS[bk].ap()[:, 0:n],
                                            cw[:, k * 128:(k + 1) * 128],
                                            XNs(k, g0, n), start=(k == 0), stop=(k == 7))
                        incs = [(sPE, 1)] if k == 7 else []
                        r = S.op("tensor", fm, waits=w, incs=incs)
                        if incs:
                            lastg[gi] = r[0]
                pw_free[slot] = [(sPE, lastg[1])]
                for gi, (g0, n) in enumerate(CG):
                    bk = (j % 2) * 2 + gi

                    def fe(e, bk=bk, j=j, g0=g0, n=n):
                        return e.activation(QKVs(j, g0, n), PS[bk].ap()[:, 0:n],
                                            AF.Identity, bias=sb_bo.ap()[:, j:j + 1])
                    ev_ = S.op("scalar", fe, waits=[(sPE, lastg[gi])],
                               incs=[(sAC, 1)])[0]
                    bank_free_w[bk] = [(sAC, ev_)]
            m_wo = (sAC, S.val(sAC))

            base = snap()
            bank_free_j = {i: list(base) for i in range(4)}
            tpk_free = [list(base), list(base)]
            proj_cd = [None] * DC
            for o in range(DC if not skip2 else 0):
                slot = pslot % 4
                pslot += 1
                cw = arb(OFF_CWR[slot], 1024)

                def fd(e, o=o, cw=cw):
                    return e.dma_start(
                        out=cw.rearrange("p (c o) -> p c o", c=8),
                        in_=projw[:, o * 128:(o + 1) * 128]
                        .rearrange("(c p) o -> p c o", p=128))
                dv_ = S.op("gpsimd", fd, waits=list(pw_free[slot]),
                           incs=[(sCW[slot], 16)])[0]
                lastg = {}
                for gi, (g0, n) in enumerate(CG):
                    bk = (o % 2) * 2 + gi
                    for k in range(8):
                        w = []
                        if k == 0:
                            w.append((sCW[slot], dv_))
                            w += list(bank_free_j[bk])
                            if o == 0 and gi == 0:
                                w.append(m_wo)

                        def fm(e, cw=cw, k=k, g0=g0, n=n, bk=bk):
                            return e.matmul(PS[bk].ap()[:, 0:n],
                                            cw[:, k * 128:(k + 1) * 128],
                                            QKVs(k, g0, n), start=(k == 0), stop=(k == 7))
                        incs = [(sPE, 1)] if k == 7 else []
                        r = S.op("tensor", fm, waits=w, incs=incs)
                        if incs:
                            lastg[gi] = r[0]
                pw_free[slot] = [(sPE, lastg[1])]
                for gi, (g0, n) in enumerate(CG):
                    bk = (o % 2) * 2 + gi
                    tpk = arf(OFF_TSB[gi % 2], 512)[:, 0:n]

                    def fe(e, bk=bk, o=o, n=n, tpk=tpk):
                        return e.activation(tpk, PS[bk].ap()[:, 0:n], AF.Identity,
                                            bias=sb_pb.ap()[:, o:o + 1])
                    a_ = S.op("scalar", fe,
                              waits=[(sPE, lastg[gi])] + list(tpk_free[gi % 2]),
                              incs=[(sAC, 1)])[0]
                    bank_free_j[bk] = [(sAC, a_)]
                    dv_ = S.op("vector", lambda e, o=o, g0=g0, n=n, tpk=tpk:
                               e.tensor_add(XRs(o, 1 + g0, n), XRs(o, 1 + g0, n), tpk),
                               waits=[(sAC, a_)], incs=[(sDV, 1)])[0]
                    tpk_free[gi % 2] = [(sDV, dv_)]
                    proj_cd[o] = (sDV, dv_)
            m_proj = (sDV, S.val(sDV))
            m_projpe = (sPE, S.val(sPE))
            del skip2
            if tap == 'proj':
                tap_src[0] = 'XR'
                tap_m[0] = m_proj

            if tap_m[0] is not None:
                m_final = tap_m[0]
            else:
                m_ics2 = S.op("gpsimd", lambda e: e.dma_start(out=icsb, in_=ics),
                              waits=[(sPE, S.val(sPE))], incs=[(sW, 16)])[0]
                m_ln3, lng3 = emit_ln(hmask=False, entry_waits=[],
                                      xr_chunk_ready=proj_cd)
                m_rt2 = emit_router(lng3)
                m_xin2 = emit_xin(entry_waits=[m_rt2, (sW, m_ics2)])
                xin2_pe = (sPE, S.val(sPE))
                ocs2_ready = []

                def load_ocs2():
                    v = S.op("gpsimd", lambda e: e.dma_start(out=icsb, in_=ocs),
                             waits=[xin2_pe], incs=[(sW, 16)])[0]
                    ocs2_ready.append((sW, v))
                m_final, _ = emit_ffn_main(first=False, groups=W2G,
                                           entry_waits=[m_xin2, m_projpe],
                                           prefetch_free=[m_projpe],
                                           ocs_load=load_ocs2, ocs_gate=ocs2_ready)
        else:
            m_final = m_ffn1

        if tap_src[0] == 'XN':
            srcfn, soff = XNs, 1
        elif tap_src[0] == 'XN0':
            srcfn, soff = XNs, 0
        else:
            srcfn, soff = XRs, 1
        ost_free = [[m_final], [m_final]]
        pb_free_o = [[m_final], [m_final]]
        for tc in range(8):
            ost = arb(OFF_OSTAGE[tc % 2], 4096)
            lastE = None
            for fcg in range(4):
                pbv = pb_bf if fcg % 2 == 0 else pb_bf6
                lastT = None
                for j in range(8):
                    fc = fcg * 8 + j
                    w = []
                    if j == 0:
                        w += list(pb_free_o[fcg % 2])
                        if fcg == 0 and tc == 0:
                            w.append(m_final)

                    def ft(e, fc=fc, j=j, tc=tc, pbv=pbv):
                        return e.transpose(pbv[:, j * 128:(j + 1) * 128],
                                           srcfn(fc, soff + tc * 128, 128),
                                           sb_id.ap())
                    r = S.op("tensor", ft, waits=w,
                             incs=[(sPE, 1)] if j == 7 else [])
                    if j == 7:
                        lastT = r[0]

                def fe(e, fcg=fcg, ost=ost, pbv=pbv):
                    return e.activation(ost[:, fcg * 1024:(fcg + 1) * 1024],
                                        pbv[:, :], AF.Copy)
                w = [(sPE, lastT)] + list(ost_free[tc % 2] if fcg == 0 else [])
                ev_ = S.op("scalar", fe, waits=w, incs=[(sAC, 1)])[0]
                pb_free_o[fcg % 2] = [(sAC, ev_)]
                lastE = ev_
            ov = S.op("gpsimd", lambda e, tc=tc, ost=ost:
                      e.dma_start(out=x_out[tc * 128:(tc + 1) * 128, :], in_=ost),
                      waits=[(sAC, lastE)], incs=[(sOS[tc % 2], 16)])[0]
            ost_free[tc % 2] = [(sOS[tc % 2], ov)]
        m_out1 = S.val(sOS[0])
        m_out2 = S.val(sOS[1])
        S.op("gpsimd", lambda e: e.nop(),
             waits=[(sOS[0], m_out1), (sOS[1], m_out2)])
        S.op("sync", lambda e: e.nop(),
             waits=[(sOS[0], m_out1), (sOS[1], m_out2)])

        names = {id(sW): 'sW', id(sHW): 'sHW', id(sPE): 'sPE', id(sDV): 'sDV',
                 id(sAC): 'sAC', id(sPL): 'sPL', id(sOUT): 'sOUT', id(sGW): 'sGW'}
        for i, s_ in enumerate(sTM):
            names[id(s_)] = f'sTM{i}'
        for i, s_ in enumerate(sWB):
            names[id(s_)] = f'sWB{i}'
        for i, s_ in enumerate(sCW):
            names[id(s_)] = f'sCW{i}'
        for i, s_ in enumerate(sOS):
            names[id(s_)] = f'sOS{i}'
        S.check(names)

        with nc.Block() as block:
            @block.sync
            def _(e):
                for fn in S.ev["sync"]:
                    fn(e)

            @block.tensor
            def _(e):
                for fn in S.ev["tensor"]:
                    fn(e)

            @block.vector
            def _(e):
                for fn in S.ev["vector"]:
                    fn(e)

            @block.scalar
            def _(e):
                for fn in S.ev["scalar"]:
                    fn(e)

            @block.gpsimd
            def _(e):
                for fn in S.ev["gpsimd"]:
                    fn(e)

    return nc


_NC_CACHE = {}


def _get_nc(upto=99, tap=None):
    key = (upto, tap)
    if key not in _NC_CACHE:
        _NC_CACHE[key] = _build_program(upto, tap)
    return _NC_CACHE[key]


def _host_inputs(L0_attn_inputs, L0_attn_conv_w, L0_attn_conv_b, L0_attn_wq, L0_attn_bq,
                 L0_attn_wk, L0_attn_bk, L0_attn_wv, L0_attn_bv, L0_attn_wo, L0_attn_bo,
                 L0_attn_proj_w, L0_attn_proj_b, L0_ffn_weight, L0_ffn_bias, L0_ffn_gate_w,
                 L0_ffn_in_channel_scale, L0_ffn_out_channel_scale):
    f32 = np.float32
    x = np.asarray(L0_attn_inputs, f32).reshape(8192, D)
    shared = dict(
        conv_w=np.ascontiguousarray(np.asarray(L0_attn_conv_w, f32)),
        convb_c=np.ascontiguousarray(np.asarray(L0_attn_conv_b, f32).reshape(24, 128).T),
        wq=np.ascontiguousarray(np.asarray(L0_attn_wq, f32).reshape(WD, WD)),
        wk=np.ascontiguousarray(np.asarray(L0_attn_wk, f32).reshape(WD, WD)),
        wv=np.ascontiguousarray(np.asarray(L0_attn_wv, f32).reshape(WD, WD)),
        wo=np.ascontiguousarray(np.asarray(L0_attn_wo, f32).reshape(WD, WD)),
        bq_c=np.ascontiguousarray(np.asarray(L0_attn_bq, f32).reshape(8, 128).T),
        bk_c=np.ascontiguousarray(np.asarray(L0_attn_bk, f32).reshape(8, 128).T),
        bv_c=np.ascontiguousarray(np.asarray(L0_attn_bv, f32).reshape(8, 128).T),
        bo_c=np.ascontiguousarray(np.asarray(L0_attn_bo, f32).reshape(8, 128).T),
        projw=np.ascontiguousarray(np.asarray(L0_attn_proj_w, f32)),
        projb_c=np.ascontiguousarray(np.asarray(L0_attn_proj_b, f32).reshape(32, 128).T),
        fwT=np.ascontiguousarray(np.asarray(L0_ffn_weight, f32).T),
        fb_c=np.ascontiguousarray(np.asarray(L0_ffn_bias, f32).reshape(32, 128).T),
        gwT3=np.ascontiguousarray(
            np.asarray(L0_ffn_gate_w, f32).T.reshape(DC, 128, 8)
            .transpose(1, 0, 2).reshape(128, 256)),
        ics=np.ascontiguousarray(np.asarray(L0_ffn_in_channel_scale, f32)),
        ocs=np.ascontiguousarray(np.asarray(L0_ffn_out_channel_scale, f32)),
    )
    in_maps = []
    for i in range(NCORES):
        t0 = i * T
        xs = np.zeros((TH, D), f32)
        lo = max(0, t0 - 1)
        hi = min(8192, t0 + T + 1)
        xs[1 + (lo - t0): 1 + (hi - t0), :] = x[lo:hi, :]
        hm = np.ones((1, TH), f32)
        if i == 0:
            hm[0, 0] = 0.0
        if i == NCORES - 1:
            hm[0, TH - 1] = 0.0
        in_maps.append(dict(x_in=xs, hmrow=hm, **shared))
    return in_maps


def kernel(**inputs):
    upto = inputs.pop("_upto", 99)
    tap = inputs.pop("_tap", None)
    in_maps = _host_inputs(**inputs)
    nc = _get_nc(upto, tap)
    res = run_bass_kernel_spmd(nc, in_maps, core_ids=list(range(NCORES)))
    out = np.concatenate([np.asarray(r["x_out"], np.float32) for r in res.results],
                         axis=0)
    return out.reshape(1, 8192, D)
